# version 07
# baseline (speedup 1.0000x reference)
import numpy as np
import concourse.bass as bass
import concourse.mybir as mybir
from concourse.bass_utils import run_bass_kernel_spmd

F32 = mybir.dt.float32
BF16 = mybir.dt.bfloat16
AF = mybir.ActivationFunctionType
ALU = mybir.AluOpType
AX = mybir.AxisListType

DIM = 4096
H = 128
P = 64
G = 8
N = 128
CS = 128
INTER = 8192
CONV_DIM = 10240
NPROJ = 18560
T = 2048
NCH = 16
NCORES = 8
EPS = 1e-5
R = 16
E = 8
SCALING = 2.0

ENGINES = ("tensor", "vector", "scalar", "gpsimd", "sync")


class Prog:

    def __init__(self):
        self.steps = []
        self.last_writer = {}
        self.readers = {}
        self.fence_deps = ()

    def step(self, engine, fn, reads=(), writes=(), kind="c", ch=None,
             label=""):
        if kind == "dma":
            assert ch is not None, label
        deps = set(self.fence_deps)
        for r in reads:
            if r in self.last_writer:
                deps.add(self.last_writer[r])
        for w in writes:
            if w in self.last_writer:
                deps.add(self.last_writer[w])
            for r in self.readers.get(w, ()):
                deps.add(r)
        i = len(self.steps)
        self.steps.append(dict(engine=engine, fn=fn, kind=kind, deps=deps,
                               label=label, idx=i, ch=ch))
        for r in reads:
            self.readers.setdefault(r, []).append(i)
        for w in writes:
            self.last_writer[w] = i
            self.readers[w] = []
        return i

    def fence(self):
        last = {}
        for s in self.steps:
            last[s["engine"], s["kind"] == "dma" and s["ch"]] = s["idx"]
        self.fence_deps = tuple(last.values())

    def sem_names(self):
        names = set()
        for s in self.steps:
            if s["kind"] == "dma":
                names.add("d_" + s["ch"])
            elif s["kind"] == "cc":
                names.add(f"cc{sum(1 for t in self.steps[:s['idx']] if t['kind'] == 'cc')}")
            else:
                names.add("c_" + s["engine"])
        return sorted(names)

    def emit(self, nc, block, sems):
        c_idx = {e: 0 for e in ENGINES}
        d_idx = {}
        cc_idx = 0
        for s in self.steps:
            e = s["engine"]
            if s["kind"] == "dma":
                k = s["ch"]
                s["seq"] = d_idx.get(k, 0)
                d_idx[k] = s["seq"] + 1
                s["sem"] = "d_" + k
                s["val"] = 16 * (s["seq"] + 1)
            elif s["kind"] == "cc":
                s["seq"] = cc_idx
                s["sem"] = f"cc{cc_idx}"
                s["val"] = 1
                cc_idx += 1
            else:
                s["seq"] = c_idx[e]
                c_idx[e] += 1
                s["sem"] = "c_" + e
                s["val"] = s["seq"] + 1
        for e in ENGINES:
            assert c_idx[e] < 60000, (e, c_idx[e])
        for k, v in d_idx.items():
            assert 16 * v < 60000, (k, v)

        clocks = [None] * len(self.steps)
        eng_clock = {e: {} for e in ENGINES}
        per_engine = {e: [] for e in ENGINES}
        for s in self.steps:
            e = s["engine"]
            need = {}
            for d in s["deps"]:
                for k, v in clocks[d].items():
                    if need.get(k, 0) < v:
                        need[k] = v
            if s["kind"] == "dma" and s["seq"] > 0:
                assert need.get(s["sem"], 0) >= 16 * s["seq"], (
                    "channel not serialized", s["label"], s["sem"], s["seq"],
                    need.get(s["sem"], 0))
            waits = []
            cur = eng_clock[e]
            for k, v in sorted(need.items()):
                if k == "c_" + e and s["kind"] != "dma" and e == "tensor":
                    continue
                if cur.get(k, 0) < v:
                    waits.append((k, v))
                    cur[k] = v
            for k, v in need.items():
                if cur.get(k, 0) < v:
                    cur[k] = v
            s["waits"] = waits
            myc = dict(cur)
            if myc.get(s["sem"], 0) < s["val"]:
                myc[s["sem"]] = s["val"]
            clocks[s["idx"]] = myc
            per_engine[e].append(s)

        for e in ENGINES:
            steps = per_engine[e]
            if not steps:
                continue

            def make_body(steps=steps):
                def body(eng):
                    for s in steps:
                        for k, v in s["waits"]:
                            eng.wait_ge(sems[k], v)
                        ins = s["fn"](eng)
                        ins.then_inc(sems[s["sem"]], 16 if s["kind"] == "dma"
                                     else 1)
                return body

            getattr(block, e)(make_body())


def build(debug=(), feed=(), stop_after=None):
    import os
    skip_cc = bool(int(os.environ.get("K_SKIP_CC", "0")))
    nc = bass.Bass("TRN2", target_bir_lowering=False)
    p = Prog()

    din = {}

    def inp(name, shape, dtype=F32):
        din[name] = nc.dram_tensor(name, list(shape), dtype, kind="ExternalInput")
        return din[name]

    x_inT = inp("x_inT", [DIM, T])
    win_t = inp("win_t", [145, 128, 32, 128])
    wout_g3 = inp("wout_g3", [8, 16, 128, 2048])
    wbase_g3 = inp("wbase_g3", [8, 8, 128, 2048])
    aw_kt = inp("aw_kt", [128, 512])
    rw_kt = inp("rw_kt", [128, 256])
    bw_t = inp("bw_t", [16, 4096])
    coresf = inp("coresf", [128, 16])
    conv_w_r = inp("conv_w_r", [128, 320])
    conv_b_r = inp("conv_b_r", [128, 80])
    dt_bias_c = inp("dt_bias_c", [128, 1])
    a_log_c = inp("a_log_c", [128, 1])
    d_exp_row = inp("d_exp_row", [1, INTER])
    d_row = inp("d_row", [1, 128])
    normw_row = inp("normw_row", [1, INTER])
    bbase_cols = inp("bbase_cols", [128, 32])
    ident_f = inp("ident_f", [128, 128])
    trilm1_f = inp("trilm1_f", [128, 128])
    triu0_f = inp("triu0_f", [128, 128])
    sel_last_f = inp("sel_last_f", [128, 128])
    ones_col = inp("ones_col", [128, 1])
    prev_cols = inp("prev_cols", [128, 4])
    inmask_cols = inp("inmask_cols", [128, 4])
    btw_row = inp("btw_row", [1, 16])

    out_t = nc.dram_tensor("out_t", [DIM, T], F32, kind="ExternalOutput")
    k1 = "ExternalInput" if "p1" in feed else "Internal"
    gateT = nc.dram_tensor("gateT", [INTER, T], BF16, kind=k1)
    xbcT = nc.dram_tensor("xbcT", [CONV_DIM, T], BF16, kind=k1)
    dtT_d = nc.dram_tensor("dtT_d", [128, T], F32, kind=k1)
    y1_tok = nc.dram_tensor("y1_tok", [T, INTER], BF16)
    y2T_d = nc.dram_tensor("y2T_d", [INTER, T], BF16)
    sloc_d = nc.dram_tensor("sloc_d", [128, INTER], F32)
    ccA_in = nc.dram_tensor("ccA_in", [128, 240], F32)
    ccA_out = nc.dram_tensor("ccA_out", [512, 240], F32)
    ccB_in = nc.dram_tensor("ccB_in", [128, 8193], F32)
    ccB_out = nc.dram_tensor("ccB_out", [512, 8193], F32)
    from contextlib import ExitStack
    ctx = ExitStack()
    with ctx:
        block = ctx.enter_context(nc.Block())

        SB = lambda n, sh, dt=F32: ctx.enter_context(nc.sbuf_tensor(n, list(sh), dt))
        PS = lambda n, sh, dt=F32: ctx.enter_context(nc.psum_tensor(n, list(sh), dt))

        g_ident_b = SB("g_ident_b", [128, 128], BF16)
        g_ident_f = SB("g_ident_f", [128, 128], F32)
        g_trilm1 = SB("g_trilm1", [128, 128], F32)
        g_triu0 = SB("g_triu0", [128, 128], F32)
        g_sel_last = SB("g_sel_last", [128, 128], F32)
        g_ones_col = SB("g_ones_col", [128, 1], F32)
        g_convw = SB("g_convw", [128, 320], F32)
        g_convb = SB("g_convb", [128, 80], F32)
        g_dtb = SB("g_dtb", [128, 1], F32)
        g_alog = SB("g_alog", [128, 1], F32)
        g_bbase = SB("g_bbase", [128, 32], F32)
        g_prev = SB("g_prev", [128, 4], F32)
        g_inmask = SB("g_inmask", [128, 4], F32)
        g_btw = SB("g_btw", [1, 16], F32)
        d_row_sb = SB("d_row_sb", [1, 128], F32)
        coresf_sb = SB("coresf_sb", [128, 16], F32)

        _ld_prev = [None]

        def ld(dst, src, eng="sync"):
            rd = (_ld_prev[0],) if _ld_prev[0] else ()
            key = "sb:" + dst.tensor.name
            p.step(eng, lambda e, d=dst, s=src: e.dma_start(out=d, in_=s),
                   reads=rd, writes=(key,), kind="dma", ch="ldg")
            _ld_prev[0] = key

        ld(g_ident_f[:, :], ident_f[:, :])
        ld(g_trilm1[:, :], trilm1_f[:, :])
        ld(g_triu0[:, :], triu0_f[:, :])
        ld(g_sel_last[:, :], sel_last_f[:, :])
        ld(g_ones_col[:, :], ones_col[:, :])
        ld(g_convw[:, :], conv_w_r[:, :])
        ld(g_convb[:, :], conv_b_r[:, :])
        ld(g_dtb[:, :], dt_bias_c[:, :])
        ld(g_alog[:, :], a_log_c[:, :])
        ld(g_bbase[:, :], bbase_cols[:, :])
        ld(g_prev[:, :], prev_cols[:, :])
        ld(g_inmask[:, :], inmask_cols[:, :])
        ld(g_btw[:, :], btw_row[:, :])
        ld(d_row_sb[:, :], d_row[:, :])
        ld(coresf_sb[:, :], coresf[:, :])
        p.step("vector", lambda e: e.tensor_copy(g_ident_b[:, :], g_ident_f[:, :]),
               reads=("sb:g_ident_f",), writes=("sb:g_ident_b",))

        ph1 = ExitStack()
        with ph1 if "p1" not in feed else ExitStack():
          if "p1" not in feed:
              xin_bf = ph1.enter_context(nc.sbuf_tensor("xin_bf", [128, 32 * T], BF16))
              xstg = [ph1.enter_context(nc.sbuf_tensor(f"xstg{i}", [128, T], F32))
                      for i in range(1)]
              wsl_f = [ph1.enter_context(nc.sbuf_tensor(f"wsl_f{i}", [128, 4096], F32))
                       for i in range(2)]
              wsl_b = [ph1.enter_context(nc.sbuf_tensor(f"wsl_b{i}", [128, 4096], BF16))
                       for i in range(2)]
              cstage = ph1.enter_context(nc.sbuf_tensor("cstage", [128, 515], F32))
              ctail = ph1.enter_context(nc.sbuf_tensor("ctail", [128, 3], F32))
              cacc = ph1.enter_context(nc.sbuf_tensor("cacc", [128, 512], F32))
              evb = [ph1.enter_context(nc.sbuf_tensor(f"evb{i}", [128, 512], BF16))
                     for i in range(4)]
              evf = [ph1.enter_context(nc.sbuf_tensor(f"evf{i}", [128, 512], F32))
                     for i in range(2)]
              cca_st = ph1.enter_context(nc.sbuf_tensor("cca_st", [128, 240], F32))
              own3 = ph1.enter_context(nc.sbuf_tensor("own3", [128, 240], F32))
              pp = [ph1.enter_context(nc.psum_tensor(f"pp{i}", [128, 2048], F32))
                    for i in range(2)]

              for kc in range(32):
                  st = xstg[0]
                  key = "xstg0"
                  p.step("sync", lambda e, st=st, kc=kc: e.dma_start(
                      out=st[:, :], in_=x_inT[kc * 128:(kc + 1) * 128, :]),
                      reads=(), writes=(key,), kind="dma", ch="xstg",
                      label=f"ldx{kc}")
                  eng = "scalar" if kc % 2 == 0 else "vector"
                  if eng == "scalar":
                      p.step(eng, lambda e, st=st, kc=kc: e.activation(
                          xin_bf[:, kc * T:(kc + 1) * T], st[:, :], AF.Identity,
                          bias=0.0, scale=1.0),
                          reads=(key,), writes=(f"xinbf:{kc}",))
                  else:
                      p.step(eng, lambda e, st=st, kc=kc: e.tensor_copy(
                          xin_bf[:, kc * T:(kc + 1) * T], st[:, :]),
                          reads=(key,), writes=(f"xinbf:{kc}",))

              xin_reads = tuple(f"xinbf:{kc}" for kc in range(32))

              for o_ in range(145):
                  sl = o_ % 2
                  pb = o_ % 2
                  wbkey = f"wslb{sl}"
                  wkey = f"wsl{sl}"
                  p.step("sync", lambda e, o_=o_, sl=sl: e.dma_start(
                      out=wsl_f[sl][:, :].rearrange("p (kc o) -> p kc o", kc=32),
                      in_=win_t.ap()[o_:o_ + 1, :, :, :].squeeze(0)),
                      reads=(wkey,), writes=(wkey,), kind="dma", ch=f"wsl{sl}",
                      label=f"ldw{o_}")
                  for hf in range(2):
                      dst = wsl_b[sl][:, hf * 2048:(hf + 1) * 2048]
                      srcv = wsl_f[sl][:, hf * 2048:(hf + 1) * 2048]
                      if hf == 0:
                          p.step("scalar", lambda e, dst=dst, srcv=srcv:
                              e.activation(dst, srcv, AF.Identity,
                                           bias=0.0, scale=1.0),
                              reads=(wkey,), writes=(wbkey,))
                      else:
                          p.step("gpsimd", lambda e, dst=dst, srcv=srcv:
                              e.tensor_copy(dst, srcv),
                              reads=(wkey,), writes=(wbkey,))

                  pkey = f"pp{pb}"
                  for kc in range(32):
                      for tt_ in range(4):
                          p.step("tensor", lambda e, pb=pb, sl=sl, kc=kc, tt_=tt_: e.matmul(
                              pp[pb][:, tt_ * 512:(tt_ + 1) * 512],
                              wsl_b[sl][:, kc * 128:(kc + 1) * 128],
                              xin_bf[:, kc * T + tt_ * 512: kc * T + (tt_ + 1) * 512],
                              start=(kc == 0), stop=(kc == 31)),
                              reads=(wbkey, f"xinbf:{kc}"), writes=(pkey,),
                              label=f"mm{o_}")

                  if o_ < 64:
                      for tt_ in range(4):
                          eb = evb[tt_ % 4]
                          ekey = f"evb{tt_ % 4}"
                          p.step("scalar", lambda e, pb=pb, tt_=tt_, eb=eb: e.activation(
                              eb[:, :], pp[pb][:, tt_ * 512:(tt_ + 1) * 512],
                              AF.Identity, bias=0.0, scale=1.0),
                              reads=(pkey,), writes=(ekey,))
                          p.step("scalar", lambda e, o_=o_, tt_=tt_, eb=eb: e.dma_start(
                              out=gateT.ap()[o_ * 128:(o_ + 1) * 128,
                                             tt_ * 512:(tt_ + 1) * 512],
                              in_=eb[:, :]),
                              reads=(ekey,), writes=(f"gateT:{o_}:{tt_}",),
                              kind="dma", ch=f"sevb{tt_ % 4}")
                  elif o_ < 144:
                      cc = o_ - 64
                      for tt_ in range(4):
                          p.step("vector", lambda e, pb=pb, tt_=tt_: e.tensor_copy(
                              cstage[:, 3:515], pp[pb][:, tt_ * 512:(tt_ + 1) * 512]),
                              reads=(pkey,), writes=("cstage",))
                          if tt_ == 0:
                              p.step("vector", lambda e: e.memset(cstage[:, 0:3], 0.0),
                                     reads=(), writes=("cstage",))
                              p.step("vector", lambda e, cc=cc: e.tensor_copy(
                                  own3[:, cc * 3:cc * 3 + 3], cstage[:, 3:6]),
                                  reads=("cstage",), writes=("own3",))
                          else:
                              p.step("vector", lambda e: e.tensor_copy(
                                  cstage[:, 0:3], ctail[:, :]),
                                  reads=("ctail",), writes=("cstage",))
                          if tt_ == 3:
                              p.step("vector", lambda e, cc=cc: e.tensor_copy(
                                  cca_st[:, cc * 3:cc * 3 + 3], cstage[:, 512:515]),
                                  reads=("cstage",), writes=("cca_st",))
                          else:
                              p.step("vector", lambda e: e.tensor_copy(
                                  ctail[:, :], cstage[:, 512:515]),
                                  reads=("cstage",), writes=("ctail",))
                          p.step("vector", lambda e, cc=cc: e.tensor_scalar(
                              cacc[:, :], cstage[:, 0:512],
                              g_convw[:, cc * 4:cc * 4 + 1], None, ALU.mult),
                              reads=("cstage", "sb:g_convw"), writes=("cacc",))
                          for j in range(1, 4):
                              eng = "vector"
                              p.step(eng, lambda e, cc=cc, j=j: e.scalar_tensor_tensor(
                                  cacc[:, :], cstage[:, j:j + 512],
                                  g_convw[:, cc * 4 + j:cc * 4 + j + 1],
                                  cacc[:, :], ALU.mult, ALU.add),
                                  reads=("cstage", "cacc", "sb:g_convw"),
                                  writes=("cacc",))
                          eb = evb[tt_ % 4]
                          ekey = f"evb{tt_ % 4}"
                          p.step("scalar", lambda e, cc=cc, eb=eb: e.activation(
                              eb[:, :], cacc[:, :], AF.Silu,
                              bias=g_convb[:, cc:cc + 1], scale=1.0),
                              reads=("cacc", "sb:g_convb"), writes=(ekey,))
                          p.step("scalar", lambda e, cc=cc, tt_=tt_, eb=eb: e.dma_start(
                              out=xbcT.ap()[cc * 128:(cc + 1) * 128,
                                            tt_ * 512:(tt_ + 1) * 512],
                              in_=eb[:, :]),
                              reads=(ekey,), writes=(f"xbcT:{cc}:{tt_}",),
                              kind="dma", ch=f"sevb{tt_ % 4}")
                  else:
                      for tt_ in range(4):
                          ef = evf[tt_ % 2]
                          ekey = f"evf{tt_ % 2}"
                          p.step("scalar", lambda e, pb=pb, tt_=tt_, ef=ef: e.activation(
                              ef[:, :], pp[pb][:, tt_ * 512:(tt_ + 1) * 512],
                              AF.Identity, bias=0.0, scale=1.0),
                              reads=(pkey,), writes=(ekey,))
                          p.step("scalar", lambda e, tt_=tt_, ef=ef: e.dma_start(
                              out=dtT_d.ap()[:, tt_ * 512:(tt_ + 1) * 512],
                              in_=ef[:, :]),
                              reads=(ekey,), writes=(f"dtT:{tt_}",), kind="dma",
                              ch=f"sevf{tt_ % 2}")

              p.step("sync", lambda e: e.dma_start(
                  out=ccA_in.ap(), in_=cca_st[:, :]),
                  reads=("cca_st",), writes=("dram:ccA_in",), kind="dma",
                  ch="ccA")

              if not skip_cc:
                  p.step("gpsimd", lambda e: e.collective_compute(
                      "AllGather", ALU.bypass,
                      replica_groups=[[0, 1, 2, 3], [4, 5, 6, 7]],
                      ins=[ccA_in.ap().opt()], outs=[ccA_out.ap().opt()]),
                      reads=("dram:ccA_in",),
                      writes=("dram:ccA_out",), kind="cc")

              gtail = ph1.enter_context(nc.sbuf_tensor("gtail", [128, 240], F32))
              gt_j = ph1.enter_context(nc.sbuf_tensor("gt_j", [128, 240], F32))
              p.step("vector", lambda e: e.memset(gtail[:, :], 0.0),
                     reads=(), writes=("gtail",))
              for j in range(4):
                  p.step("sync", lambda e, j=j: e.dma_start(
                      out=gt_j[:, :],
                      in_=ccA_out.ap()[j * 128:(j + 1) * 128, :]),
                      reads=("dram:ccA_out",), writes=("gt_j",), kind="dma",
                      ch="gtj")
                  p.step("vector", lambda e, j=j: e.scalar_tensor_tensor(
                      gtail[:, :], gt_j[:, :], g_prev[:, j:j + 1], gtail[:, :],
                      ALU.mult, ALU.add),
                      reads=("gt_j", "sb:g_prev", "gtail"), writes=("gtail",))

              p6 = ph1.enter_context(nc.sbuf_tensor("p6", [128, 6], F32))
              pacc3 = ph1.enter_context(nc.sbuf_tensor("pacc3", [128, 3], F32))
              pout3 = ph1.enter_context(nc.sbuf_tensor("pout3", [128, 3], BF16))
              for cc in range(80):
                  p.step("vector", lambda e, cc=cc: e.tensor_copy(
                      p6[:, 0:3], gtail[:, cc * 3:cc * 3 + 3]),
                      reads=("gtail",), writes=("p6",))
                  p.step("vector", lambda e, cc=cc: e.tensor_copy(
                      p6[:, 3:6], own3[:, cc * 3:cc * 3 + 3]),
                      reads=("own3",), writes=("p6",))
                  p.step("vector", lambda e, cc=cc: e.tensor_scalar(
                      pacc3[:, :], p6[:, 0:3],
                      g_convw[:, cc * 4:cc * 4 + 1], None, ALU.mult),
                      reads=("p6", "sb:g_convw"), writes=("pacc3",))
                  for j in range(1, 4):
                      p.step("vector", lambda e, cc=cc, j=j: e.scalar_tensor_tensor(
                          pacc3[:, :], p6[:, j:j + 3],
                          g_convw[:, cc * 4 + j:cc * 4 + j + 1],
                          pacc3[:, :], ALU.mult, ALU.add),
                          reads=("p6", "pacc3", "sb:g_convw"), writes=("pacc3",))
                  p.step("scalar", lambda e, cc=cc: e.activation(
                      pout3[:, :], pacc3[:, :], AF.Silu,
                      bias=g_convb[:, cc:cc + 1], scale=1.0),
                      reads=("pacc3", "sb:g_convb"), writes=("pout3",))
                  p.step("scalar", lambda e, cc=cc: e.dma_start(
                      out=xbcT.ap()[cc * 128:(cc + 1) * 128, 0:3],
                      in_=pout3[:, :]),
                      reads=("pout3",), writes=(f"xbcT:{cc}:0",), kind="dma",
                      ch="spatch")

        p.fence()

        ph2 = ExitStack()
        with ph2:
            S2 = lambda n, sh, dt=F32: ph2.enter_context(
                nc.sbuf_tensor(n, list(sh), dt))
            f_dt = S2("f_dt", [128, T])
            f_adt = S2("f_adt", [128, T])
            f_acum = S2("f_acum", [128, T])
            f_wst = S2("f_wst", [128, T])
            cmask = S2("cmask", [128, T])
            a_col = S2("a_col", [128, 1])
            aseg = S2("aseg", [128, 1])
            Pst = S2("P", [128, INTER])
            P_bf = S2("P_bf", [128, INTER], BF16)
            x_tok = S2("x_tok", [128, INTER], BF16)
            x_st = S2("x_st", [128, INTER], BF16)
            y_ch = S2("y_ch", [128, INTER], BF16)
            bc_sb = S2("bc_sb", [128, 2048], BF16)
            btok = S2("btok", [128, 1024], BF16)
            gmask = S2("gmask", [128, 1024], BF16)
            lhsb = S2("lhsb", [128, 2048], F32)
            t2 = S2("t2", [128, 2048], BF16)
            mbat = S2("mbat", [128, 2048], BF16)
            tok_f = S2("tok_f", [128, 512], F32)
            e_tok = S2("e_tok", [128, 128], F32)
            drep_c = S2("drep_c", [128, 128], F32)
            d_rep = S2("d_rep", [128, 128], F32)
            t_off = S2("t_off", [128, 1024], BF16)
            t_off2 = S2("t_off2", [128, 1024], BF16)
            ptmp = S2("ptmp", [128, 512], F32)
            pyD = ph2.enter_context(nc.psum_tensor("pyD", [128, 1024], F32))
            pyO = ph2.enter_context(nc.psum_tensor("pyO", [128, 1024], F32))
            pst = ph2.enter_context(nc.psum_tensor("pst", [128, 512], F32))
            pdiff = ph2.enter_context(nc.psum_tensor("pdiff", [128, 512], F32))
            pwk_f = ph2.enter_context(nc.psum_tensor("pwk_f", [128, 512], F32))
            pwk_b = ph2.enter_context(nc.psum_tensor("pwk_b", [128, 1024], BF16))

            st = p.step

            for tt_ in range(4):
                st("sync", lambda e, tt_=tt_: e.dma_start(
                    out=f_dt[:, tt_ * 512:(tt_ + 1) * 512],
                    in_=dtT_d.ap()[:, tt_ * 512:(tt_ + 1) * 512]),
                    reads=(f"dtT:{tt_}",), writes=("f_dt",), kind="dma",
                    ch="ldt")
            st("vector", lambda e: e.tensor_scalar(
                f_dt[:, :], f_dt[:, :], g_dtb[:, :], None, ALU.add),
                reads=("f_dt", "sb:g_dtb"), writes=("f_dt",))
            st("scalar", lambda e: e.activation(
                f_adt[:, :], f_dt[:, :], AF.Abs, bias=0.0, scale=1.0),
                reads=("f_dt",), writes=("f_adt",))
            st("scalar", lambda e: e.activation(
                f_adt[:, :], f_adt[:, :], AF.Exp, bias=0.0, scale=-1.0),
                reads=("f_adt",), writes=("f_adt",))
            st("scalar", lambda e: e.activation(
                f_adt[:, :], f_adt[:, :], AF.Ln, bias=1.0, scale=1.0),
                reads=("f_adt",), writes=("f_adt",))
            st("vector", lambda e: e.tensor_scalar(
                f_wst[:, :], f_dt[:, :], 0.0, None, ALU.max),
                reads=("f_dt",), writes=("f_wst",))
            st("vector", lambda e: e.tensor_add(
                f_dt[:, :], f_wst[:, :], f_adt[:, :]),
                reads=("f_wst", "f_adt"), writes=("f_dt",))
            st("scalar", lambda e: e.activation(
                a_col[:, :], g_alog[:, :], AF.Exp, bias=0.0, scale=1.0),
                reads=("sb:g_alog",), writes=("a_col",))
            st("vector", lambda e: e.tensor_scalar(
                a_col[:, :], a_col[:, :], -1.0, None, ALU.mult),
                reads=("a_col",), writes=("a_col",))
            st("vector", lambda e: e.tensor_scalar(
                f_adt[:, :], f_dt[:, :], a_col[:, :], None, ALU.mult),
                reads=("f_dt", "a_col"), writes=("f_adt",))
            st("vector", lambda e: e.memset(cmask[:, :], 1.0),
                reads=(), writes=("cmask",))
            st("vector", lambda e: e.memset(cmask[:, 0:T:CS], 0.0),
                reads=(), writes=("cmask",))
            st("vector", lambda e: e.tensor_tensor_scan(
                f_acum[:, :], cmask[:, :], f_adt[:, :], 0.0,
                ALU.mult, ALU.add),
                reads=("cmask", "f_adt"), writes=("f_acum",))
            endv = f_acum[:, CS - 1:T:CS]
            st("vector", lambda e: e.tensor_tensor(
                f_wst[:, :].rearrange("p (c l) -> p c l", c=NCH),
                f_acum[:, :].rearrange("p (c l) -> p c l", c=NCH),
                endv[:, :, None].broadcast_to((128, NCH, CS)),
                ALU.subtract),
                reads=("f_acum",), writes=("f_wst",))
            st("scalar", lambda e: e.activation(
                f_wst[:, :], f_wst[:, :], AF.Exp, bias=0.0, scale=-1.0),
                reads=("f_wst",), writes=("f_wst",))
            st("vector", lambda e: e.tensor_mul(
                f_wst[:, :], f_wst[:, :], f_dt[:, :]),
                reads=("f_wst", "f_dt"), writes=("f_wst",))
            st("vector", lambda e: e.reduce_sum(
                aseg[:, :], f_acum[:, CS - 1:T:CS], axis=AX.X),
                reads=("f_acum",), writes=("aseg",))
            st("tensor", lambda e: e.matmul(
                pwk_f[:, 0:128], g_triu0[0:1, :], d_row_sb[:, :],
                start=True, stop=True),
                reads=("sb:g_triu0", "sb:d_row_sb"), writes=("pwk_f",))
            st("scalar", lambda e: e.activation(
                d_rep[:, :], pwk_f[:, 0:128], AF.Identity, bias=0.0,
                scale=1.0),
                reads=("pwk_f",), writes=("d_rep",))
            st("vector", lambda e: e.memset(Pst[:, :], 0.0),
                reads=(), writes=("P",))

            xre = lambda a: a.rearrange("p (h q) -> p h q", h=H)

            for c in range(NCH):
                csl = slice(c * CS, (c + 1) * CS)
                tt_c = c // 4
                xkeys = tuple(f"xbcT:{cc}:{tt_c}" for cc in range(64))
                if tt_c == 0:
                    xkeys = xkeys + tuple(f"xbcT:{cc}:0" for cc in range(64))
                bckeys = tuple(f"xbcT:{cc}:{tt_c}" for cc in range(64, 80)) + (
                    tuple(f"xbcT:{cc}:0" for cc in range(64, 80)) if c == 0
                    else ())
                st("sync", lambda e, c=c: e.dma_start_transpose(
                    x_tok[:, :], xbcT.ap()[0:INTER, c * CS:(c + 1) * CS]),
                    reads=xkeys, writes=("x_tok",), kind="dma", ch="xtok",
                    label=f"xtok{c}")
                st("scalar", lambda e, c=c: e.dma_start(
                    out=bc_sb[:, 0:1024].rearrange("p (g l) -> p g l", g=8),
                    in_=xbcT.ap()[INTER:INTER + 1024, c * CS:(c + 1) * CS]
                    .rearrange("(g p) l -> p g l", g=8)),
                    reads=bckeys, writes=("bc",), kind="dma", ch="bcB",
                    label=f"bcB{c}")
                st("scalar", lambda e, c=c: e.dma_start(
                    out=bc_sb[:, 1024:2048].rearrange("p (g l) -> p g l", g=8),
                    in_=xbcT.ap()[INTER + 1024:INTER + 2048,
                                  c * CS:(c + 1) * CS]
                    .rearrange("(g p) l -> p g l", g=8)),
                    reads=bckeys, writes=("bc",), kind="dma", ch="bcC",
                    label=f"bcC{c}")

                for i, f in enumerate((f_adt, f_dt, f_wst, f_acum)):
                    st("tensor", lambda e, i=i, f=f, csl=csl: e.transpose(
                        pwk_f[:, i * 128:(i + 1) * 128], f[:, csl],
                        g_ident_f[:, :]),
                        reads=("f_adt", "f_dt", "f_wst", "f_acum",
                               "sb:g_ident_f"),
                        writes=("pwk_f",))
                st("vector", lambda e: e.tensor_copy(tok_f[:, :], pwk_f[:, :]),
                    reads=("pwk_f",), writes=("tok_f",))
                st("scalar", lambda e: e.activation(
                    e_tok[:, :], tok_f[:, 384:512], AF.Exp, bias=0.0,
                    scale=1.0),
                    reads=("tok_f",), writes=("e_tok",))
                st("tensor", lambda e: e.matmul(
                    pwk_f[:, 0:128], g_sel_last[:, :], tok_f[:, 384:512],
                    start=True, stop=True),
                    reads=("sb:g_sel_last", "tok_f"), writes=("pwk_f",))
                st("scalar", lambda e: e.activation(
                    drep_c[:, :], pwk_f[:, 0:128], AF.Exp, bias=0.0,
                    scale=1.0),
                    reads=("pwk_f",), writes=("drep_c",))

                for g in range(8):
                    st("tensor", lambda e, g=g: e.transpose(
                        pwk_b[:, g * 128:(g + 1) * 128],
                        bc_sb[:, g * 128:(g + 1) * 128], g_ident_b[:, :]),
                        reads=("bc", "sb:g_ident_b"), writes=("pwk_b",))
                st("vector", lambda e: e.tensor_copy(btok[:, :], pwk_b[:, :]),
                    reads=("pwk_b",), writes=("btok",))

                for rnd in range(2):
                    for gg in range(4):
                        g = rnd * 4 + gg
                        st("tensor", lambda e, g=g, gg=gg: e.matmul(
                            pwk_f[:, gg * 128:(gg + 1) * 128],
                            bc_sb[:, g * 128:(g + 1) * 128],
                            bc_sb[:, 1024 + g * 128:1024 + (g + 1) * 128],
                            start=True, stop=True),
                            reads=("bc",), writes=("pwk_f",))
                    st("vector", lambda e, rnd=rnd: e.tensor_tensor(
                        gmask[:, rnd * 512:(rnd + 1) * 512]
                        .rearrange("p (g l) -> p g l", g=4),
                        pwk_f[:, :].rearrange("p (g l) -> p g l", g=4),
                        g_triu0[:, :, None].transpose([0, 2, 1])
                        .broadcast_to((128, 4, 128)),
                        ALU.mult),
                        reads=("pwk_f", "sb:g_triu0"), writes=("gmask",))

                st("gpsimd", lambda e: e.tensor_tensor(
                    xre(x_st[:, :]), xre(x_tok[:, :]),
                    tok_f[:, 256:384][:, :, None].broadcast_to((128, H, P)),
                    ALU.mult),
                    reads=("x_tok", "tok_f"), writes=("x_st",))
                st("scalar", lambda e: e.activation(
                    P_bf[:, :], Pst[:, :], AF.Identity, bias=0.0, scale=1.0),
                    reads=("P",), writes=("P_bf",))

                for g in range(8):
                    for hh in range(2):
                        h0 = g * 16 + hh * 8
                        st("gpsimd", lambda e, h0=h0, hh=hh: e.tensor_tensor(
                            lhsb[:, hh * 1024:(hh + 1) * 1024]
                            .rearrange("p (h s) -> p h s", h=8),
                            g_trilm1[:, None, :].broadcast_to((128, 8, 128)),
                            tok_f[:, h0:h0 + 8][:, :, None]
                            .broadcast_to((128, 8, 128)),
                            ALU.mult),
                            reads=("tok_f", "sb:g_trilm1"),
                            writes=(f"lhsb:{hh}",))
                    for rnd in range(4):
                        for i in range(4):
                            hl = rnd * 4 + i
                            st("tensor", lambda e, hl=hl, i=i: e.matmul(
                                pdiff[:, i * 128:(i + 1) * 128],
                                lhsb[:, hl * 128:(hl + 1) * 128],
                                g_triu0[:, :], start=True, stop=True),
                                reads=(f"lhsb:{hl // 8}", "sb:g_triu0"),
                                writes=("pdiff",))
                        st("scalar", lambda e, rnd=rnd: e.activation(
                            t2[:, rnd * 512:(rnd + 1) * 512], pdiff[:, :],
                            AF.Exp, bias=0.0, scale=1.0),
                            reads=("pdiff",), writes=("t2",))
                    st("vector", lambda e, g=g: e.tensor_tensor(
                        mbat[:, :].rearrange("p (h l) -> p h l", h=16),
                        t2[:, :].rearrange("p (h l) -> p h l", h=16),
                        tok_f[:, 128 + g * 16:128 + (g + 1) * 16][:, :, None]
                        .broadcast_to((128, 16, CS)),
                        ALU.mult),
                        reads=("t2", "tok_f"), writes=("mbat",))
                    st("vector", lambda e, g=g: e.tensor_tensor(
                        mbat[:, :].rearrange("p (h l) -> p h l", h=16),
                        mbat[:, :].rearrange("p (h l) -> p h l", h=16),
                        gmask[:, g * 128:(g + 1) * 128][:, None, :]
                        .broadcast_to((128, 16, CS)),
                        ALU.mult),
                        reads=("mbat", "gmask"), writes=("mbat",))

                    for i in range(16):
                        h = g * 16 + i
                        st("tensor", lambda e, i=i, h=h: e.matmul(
                            pyD[:, i * 64:(i + 1) * 64],
                            mbat[:, i * 128:(i + 1) * 128],
                            x_tok[:, h * 64:(h + 1) * 64],
                            start=True, stop=True),
                            reads=("mbat", "x_tok"), writes=("pyD",))
                        st("tensor", lambda e, i=i, h=h, g=g: e.matmul(
                            pyO[:, i * 64:(i + 1) * 64],
                            bc_sb[:, 1024 + g * 128:1024 + (g + 1) * 128],
                            P_bf[:, h * 64:(h + 1) * 64],
                            start=True, stop=True),
                            reads=("bc", "P_bf"), writes=("pyO",))
                    for rnd in range(2):
                        for i in range(8):
                            h = g * 16 + rnd * 8 + i
                            st("tensor", lambda e, i=i, h=h, g=g: e.matmul(
                                pst[:, i * 64:(i + 1) * 64],
                                btok[:, g * 128:(g + 1) * 128],
                                x_st[:, h * 64:(h + 1) * 64],
                                start=True, stop=True),
                                reads=("btok", "x_st"), writes=("pst",))
                        h0 = g * 16 + rnd * 8
                        st("gpsimd", lambda e, h0=h0: e.tensor_tensor(
                            ptmp[:, :].rearrange("p (h q) -> p h q", h=8),
                            Pst[:, h0 * 64:(h0 + 8) * 64]
                            .rearrange("p (h q) -> p h q", h=8),
                            drep_c[:, h0:h0 + 8][:, :, None]
                            .broadcast_to((128, 8, P)),
                            ALU.mult),
                            reads=("P", "drep_c"), writes=("ptmp",))
                        st("vector", lambda e, h0=h0: e.tensor_add(
                            Pst[:, h0 * 64:(h0 + 8) * 64], ptmp[:, :],
                            pst[:, :]),
                            reads=("ptmp", "pst"), writes=("P",))

                    st("vector", lambda e, g=g: e.tensor_tensor(
                        t_off[:, :].rearrange("p (h q) -> p h q", h=16),
                        pyO[:, :].rearrange("p (h q) -> p h q", h=16),
                        e_tok[:, g * 16:(g + 1) * 16][:, :, None]
                        .broadcast_to((128, 16, P)),
                        ALU.mult),
                        reads=("pyO", "e_tok"), writes=("t_off",))
                    st("gpsimd", lambda e, g=g: e.tensor_tensor(
                        t_off2[:, :].rearrange("p (h q) -> p h q", h=16),
                        x_tok[:, g * 1024:(g + 1) * 1024]
                        .rearrange("p (h q) -> p h q", h=16),
                        d_rep[:, g * 16:(g + 1) * 16][:, :, None]
                        .broadcast_to((128, 16, P)),
                        ALU.mult),
                        reads=("x_tok", "d_rep"), writes=("t_off2",))
                    st("vector", lambda e, g=g: e.tensor_add(
                        y_ch[:, g * 1024:(g + 1) * 1024], t_off[:, :],
                        pyD[:, :]),
                        reads=("t_off", "pyD"), writes=("y_ch",))
                    st("gpsimd", lambda e, g=g: e.tensor_add(
                        y_ch[:, g * 1024:(g + 1) * 1024],
                        y_ch[:, g * 1024:(g + 1) * 1024], t_off2[:, :]),
                        reads=("y_ch", "t_off2"), writes=("y_ch",))

                st("sync", lambda e, c=c: e.dma_start(
                    out=y1_tok.ap()[c * CS:(c + 1) * CS, :], in_=y_ch[:, :]),
                    reads=("y_ch",), writes=(f"y1:{c}",), kind="dma",
                    ch="sy1")

            st("sync", lambda e: e.dma_start(
                out=ccB_in.ap()[:, 0:INTER], in_=Pst[:, :]),
                reads=("P",), writes=("dram:ccB_in",), kind="dma", ch="ccB")
            st("sync", lambda e: e.dma_start(
                out=ccB_in.ap()[:, INTER:INTER + 1], in_=aseg[:, :],
                allow_slow_non_contiguous=True),
                reads=("aseg", "dram:ccB_in"), writes=("dram:ccB_in2",),
                kind="dma", ch="ccB")
            st("sync", lambda e: e.dma_start(
                out=sloc_d.ap()[:, :], in_=Pst[:, :]),
                reads=("P",), writes=("dram:sloc",), kind="dma", ch="sdbg")

            normw_rep = S2("normw_rep", [128, INTER], BF16)
            edec = S2("edec", [128, 128], F32)
            epre = [S2(f"epre{i}", [128, 128], F32) for i in range(2)]
            var_c = S2("var_c", [128, 1], F32)
            rstd_c = S2("rstd_c", [128, 1], F32)

            if not skip_cc:
                st("gpsimd", lambda e: e.collective_compute(
                    "AllGather", ALU.bypass,
                    replica_groups=[[0, 1, 2, 3], [4, 5, 6, 7]],
                    ins=[ccB_in.ap().opt()], outs=[ccB_out.ap().opt()]),
                    reads=("dram:ccB_in", "dram:ccB_in2"),
                    writes=("dram:ccB_out",), kind="cc")

            for i in range(16):
                st("sync", lambda e, i=i: e.dma_start(
                    out=ptmp[0:1, :],
                    in_=normw_row.ap()[:, i * 512:(i + 1) * 512]),
                    reads=(), writes=("ptmp",), kind="dma", ch="lnw")
                st("tensor", lambda e: e.matmul(
                    pwk_f[:, :], g_triu0[0:1, :], ptmp[0:1, :],
                    start=True, stop=True),
                    reads=("ptmp", "sb:g_triu0"), writes=("pwk_f",))
                st("scalar", lambda e, i=i: e.activation(
                    normw_rep[:, i * 512:(i + 1) * 512], pwk_f[:, :],
                    AF.Identity, bias=0.0, scale=1.0),
                    reads=("pwk_f",), writes=("normw_rep",))

            for j in range(4):
                st("sync", lambda e, j=j: e.dma_start(
                    out=tok_f[0:1, j * 128:(j + 1) * 128],
                    in_=ccB_out.ap()[j * 128:(j + 1) * 128,
                                     INTER:INTER + 1].transpose([1, 0]),
                    allow_slow_non_contiguous=True),
                    reads=("dram:ccB_out",), writes=("tok_f",), kind="dma",
                    ch="lasg")

            st("vector", lambda e: e.memset(Pst[:, :], 0.0),
                reads=(), writes=("P",))
            sj_st = (f_dt, f_adt, f_wst, cmask)
            for j in range(4):
                st("vector", lambda e: e.memset(e_tok[0:1, :], 0.0),
                    reads=(), writes=("e_tok",))
                for i in range(4):
                    st("vector", lambda e, j=j, i=i: e.scalar_tensor_tensor(
                        e_tok[0:1, :], tok_f[0:1, i * 128:(i + 1) * 128],
                        g_btw[:, j * 4 + i:j * 4 + i + 1], e_tok[0:1, :],
                        ALU.mult, ALU.add),
                        reads=("tok_f", "sb:g_btw", "e_tok"),
                        writes=("e_tok",))
                st("scalar", lambda e: e.activation(
                    e_tok[0:1, :], e_tok[0:1, :], AF.Exp, bias=0.0,
                    scale=1.0),
                    reads=("e_tok",), writes=("e_tok",))
                st("tensor", lambda e: e.matmul(
                    pwk_f[:, 0:128], g_triu0[0:1, :], e_tok[0:1, :],
                    start=True, stop=True),
                    reads=("e_tok", "sb:g_triu0"), writes=("pwk_f",))
                st("scalar", lambda e: e.activation(
                    edec[:, :], pwk_f[:, 0:128], AF.Identity, bias=0.0,
                    scale=1.0),
                    reads=("pwk_f",), writes=("edec",))
                for q in range(4):
                    sb_ = sj_st[q]
                    skey = ("f_dt", "f_adt", "f_wst", "cmask")[q]
                    st("sync", lambda e, j=j, q=q, sb_=sb_: e.dma_start(
                        out=sb_[:, :],
                        in_=ccB_out.ap()[j * 128:(j + 1) * 128,
                                         q * 2048:(q + 1) * 2048]),
                        reads=("dram:ccB_out",), writes=(skey,), kind="dma",
                        ch=f"lsj{q}")
                    st("gpsimd", lambda e, q=q, sb_=sb_: e.tensor_tensor(
                        sb_[:, :].rearrange("p (h x) -> p h x", h=32),
                        sb_[:, :].rearrange("p (h x) -> p h x", h=32),
                        edec[:, q * 32:(q + 1) * 32][:, :, None]
                        .broadcast_to((128, 32, P)),
                        ALU.mult),
                        reads=(skey, "edec"), writes=(skey,))
                    st("vector", lambda e, j=j, q=q, sb_=sb_: e.scalar_tensor_tensor(
                        Pst[:, q * 2048:(q + 1) * 2048], sb_[:, :],
                        g_inmask[:, j:j + 1],
                        Pst[:, q * 2048:(q + 1) * 2048],
                        ALU.mult, ALU.add),
                        reads=(skey, "sb:g_inmask", "P"), writes=("P",))

            st("vector", lambda e: e.memset(epre[0][:, :], 1.0),
                reads=(), writes=("epre0",))

            for c in range(NCH):
                pe_i, pe_o = epre[c % 2], epre[(c + 1) % 2]
                tt_c = c // 4
                bckeys = tuple(f"xbcT:{cc}:{tt_c}" for cc in range(72, 80))
                if tt_c == 0:
                    bckeys = bckeys + tuple(
                        f"xbcT:{cc}:0" for cc in range(72, 80))
                st("scalar", lambda e, c=c: e.dma_start(
                    out=bc_sb[:, 1024:2048].rearrange("p (g l) -> p g l", g=8),
                    in_=xbcT.ap()[INTER + 1024:INTER + 2048,
                                  c * CS:(c + 1) * CS]
                    .rearrange("(g p) l -> p g l", g=8)),
                    reads=bckeys, writes=("bc",), kind="dma", ch="bcC",
                    label=f"bcC2_{c}")
                st("sync", lambda e, c=c: e.dma_start(
                    out=y_ch[:, :], in_=y1_tok.ap()[c * CS:(c + 1) * CS, :]),
                    reads=(f"y1:{c}",), writes=("y_ch",), kind="dma",
                    ch="ly1", label=f"ly1_{c}")
                st("scalar", lambda e, c=c: e.dma_start_transpose(
                    x_tok[:, :], gateT.ap()[0:INTER, c * CS:(c + 1) * CS]),
                    reads=tuple(f"gateT:{o_}:{tt_c}" for o_ in range(64)),
                    writes=("x_tok",), kind="dma", ch="lgate",
                    label=f"lgate{c}")
                st("scalar", lambda e: e.activation(
                    x_st[:, :], x_tok[:, :], AF.Silu, bias=0.0, scale=1.0),
                    reads=("x_tok",), writes=("x_st",))
                st("tensor", lambda e, c=c: e.transpose(
                    pwk_f[:, 384:512], f_acum[:, c * CS:(c + 1) * CS],
                    g_ident_f[:, :]),
                    reads=("f_acum", "sb:g_ident_f"), writes=("pwk_f",))
                st("vector", lambda e: e.tensor_copy(
                    tok_f[:, 384:512], pwk_f[:, 384:512]),
                    reads=("pwk_f",), writes=("tok_f",))
                st("scalar", lambda e: e.activation(
                    e_tok[:, :], tok_f[:, 384:512], AF.Exp, bias=0.0,
                    scale=1.0),
                    reads=("tok_f",), writes=("e_tok",))
                st("tensor", lambda e: e.matmul(
                    pwk_f[:, 0:128], g_sel_last[:, :], tok_f[:, 384:512],
                    start=True, stop=True),
                    reads=("sb:g_sel_last", "tok_f"), writes=("pwk_f",))
                st("scalar", lambda e: e.activation(
                    drep_c[:, :], pwk_f[:, 0:128], AF.Exp, bias=0.0,
                    scale=1.0),
                    reads=("pwk_f",), writes=("drep_c",))
                st("vector", lambda e, pe_i=pe_i: e.tensor_tensor(
                    xre(P_bf[:, :]), xre(Pst[:, :]),
                    pe_i[:, :][:, :, None].broadcast_to((128, H, P)),
                    ALU.mult),
                    reads=("P", f"epre{c % 2}"), writes=("P_bf",))
                st("gpsimd", lambda e, pe_i=pe_i, pe_o=pe_o: e.tensor_mul(
                    pe_o[:, :], pe_i[:, :], drep_c[:, :]),
                    reads=(f"epre{c % 2}", "drep_c"),
                    writes=(f"epre{(c + 1) % 2}",))

                for g in range(8):
                    for i in range(16):
                        h = g * 16 + i
                        st("tensor", lambda e, i=i, h=h, g=g: e.matmul(
                            pyO[:, i * 64:(i + 1) * 64],
                            bc_sb[:, 1024 + g * 128:1024 + (g + 1) * 128],
                            P_bf[:, h * 64:(h + 1) * 64],
                            start=True, stop=True),
                            reads=("bc", "P_bf"), writes=("pyO",))
                    st("vector", lambda e, g=g: e.tensor_tensor(
                        t_off[:, :].rearrange("p (h q) -> p h q", h=16),
                        pyO[:, :].rearrange("p (h q) -> p h q", h=16),
                        e_tok[:, g * 16:(g + 1) * 16][:, :, None]
                        .broadcast_to((128, 16, P)),
                        ALU.mult),
                        reads=("pyO", "e_tok"), writes=("t_off",))
                    st("gpsimd", lambda e, g=g: e.tensor_add(
                        y_ch[:, g * 1024:(g + 1) * 1024],
                        y_ch[:, g * 1024:(g + 1) * 1024], t_off[:, :]),
                        reads=("y_ch", "t_off"), writes=("y_ch",))

                st("scalar", lambda e: e.activation(
                    mbat[:, :], y_ch[:, 0:2048], AF.Square, bias=0.0,
                    scale=1.0, accum_out=var_c[:, :]),
                    reads=("y_ch",), writes=("mbat", "var_c"))
                for qq in range(1, 4):
                    st("scalar", lambda e, qq=qq: e.activation(
                        mbat[:, :], y_ch[:, qq * 2048:(qq + 1) * 2048],
                        AF.Square, bias=0.0, scale=1.0,
                        accum_out=rstd_c[:, :]),
                        reads=("y_ch",), writes=("mbat", "rstd_c"))
                    st("vector", lambda e: e.tensor_add(
                        var_c[:, :], var_c[:, :], rstd_c[:, :]),
                        reads=("var_c", "rstd_c"), writes=("var_c",))
                st("vector", lambda e: e.tensor_scalar(
                    var_c[:, :], var_c[:, :], 1.0 / INTER, EPS,
                    ALU.mult, ALU.add),
                    reads=("var_c",), writes=("var_c",))
                st("scalar", lambda e: e.activation(
                    var_c[:, :], var_c[:, :], AF.Sqrt, bias=0.0, scale=1.0),
                    reads=("var_c",), writes=("var_c",))
                st("vector", lambda e: e.reciprocal(rstd_c[:, :], var_c[:, :]),
                    reads=("var_c",), writes=("rstd_c",))

                for g in range(8):
                    st("vector", lambda e, g=g: e.scalar_tensor_tensor(
                        t_off[:, :], y_ch[:, g * 1024:(g + 1) * 1024],
                        rstd_c[:, :], normw_rep[:, g * 1024:(g + 1) * 1024],
                        ALU.mult, ALU.mult),
                        reads=("y_ch", "rstd_c", "normw_rep"),
                        writes=("t_off",))
                    st("gpsimd", lambda e, g=g: e.tensor_mul(
                        y_ch[:, g * 1024:(g + 1) * 1024], t_off[:, :],
                        x_st[:, g * 1024:(g + 1) * 1024]),
                        reads=("t_off", "x_st"), writes=("y_ch",))

                st("sync", lambda e, c=c: e.dma_start(
                    out=y1_tok.ap()[c * CS:(c + 1) * CS, :], in_=y_ch[:, :]),
                    reads=("y_ch",), writes=(f"y2:{c}",), kind="dma",
                    ch="sy1")

            ytslots = [(t2, 0), (t2, 1024), (mbat, 0), (mbat, 1024)]
            for th in range(2):
                ykeys = tuple(f"y2:{c}" for c in range(th * 8, th * 8 + 8))
                for kc in range(64):
                    buf, off = ytslots[kc % 4]
                    bkey = "t2" if kc % 4 < 2 else "mbat"
                    skey = f"yts{kc % 4}"
                    rch = f"yt{kc % 4}"
                    reng = "scalar" if kc % 2 == 0 else "sync"
                    st(reng, lambda e, th=th, kc=kc, buf=buf, off=off:
                        e.dma_start_transpose(
                            buf[:, off:off + 1024],
                            y1_tok.ap()[th * 1024:(th + 1) * 1024,
                                        kc * 128:(kc + 1) * 128]),
                        reads=ykeys, writes=(skey, bkey), kind="dma", ch=rch)
                    st(reng, lambda e, th=th, kc=kc, buf=buf, off=off:
                        e.dma_start(
                            out=y2T_d.ap()[kc * 128:(kc + 1) * 128,
                                           th * 1024:(th + 1) * 1024],
                            in_=buf[:, off:off + 1024]),
                        reads=(skey, bkey), writes=(f"y2T:{kc}:{th}",),
                        kind="dma", ch=rch)

        p.fence()

        ph3 = ExitStack()
        with ph3:
            S3 = lambda n, sh, dt=F32: ph3.enter_context(
                nc.sbuf_tensor(n, list(sh), dt))
            obuf = S3("obuf", [128, 32 * 1024], BF16)
            x1buf = S3("x1buf", [128, 32 * 1024], BF16)
            wk_f = [S3(f"wk_f{i}", [128, 2048]) for i in range(2)]
            wk_b = [S3(f"wk_b{i}", [128, 2048], BF16) for i in range(2)]
            yr = [S3(f"yr{i}", [128, 4096], BF16) for i in range(2)]
            x0s = [S3(f"x0s{i}", [128, 512]) for i in range(2)]
            sqb = [S3(f"sqb{i}", [128, 512], BF16) for i in range(2)]
            ones_b = S3("ones_b", [128, 1], BF16)
            m_row = S3("m_row", [1, 512])
            v_row = S3("v_row", [1, 512])
            t_row = S3("t_row", [1, 512])
            awb = S3("awb", [128, 512], BF16)
            rwb = S3("rwb", [128, 256], BF16)
            aws = wk_f[0][:, 0:512]
            rws = wk_f[0][:, 512:768]
            axrt = S3("axrt", [40, 1024])
            axf = axrt[0:16, :]
            rtf = axrt[32:40, :]
            ptok = S3("ptok", [128, 64])
            pcol = S3("pcol", [128, 1])
            atok = S3("atok", [128, 128])
            w2b = S3("w2b", [128, 128])
            w2ts = S3("w2ts", [128, 128], BF16)
            mixb = S3("mixb", [16, 1024], BF16)
            coresb = S3("coresb", [128, 16], BF16)
            bwb = S3("bwb", [16, 4096], BF16)
            outs = [S3(f"outs{i}", [128, 512]) for i in range(2)]
            pmm = [ph3.enter_context(nc.psum_tensor(f"pmm{i}", [128, 1024], F32))
                   for i in range(4)]
            st = p.step

            st("vector", lambda e: e.tensor_copy(ones_b[:, :], g_ones_col[:, :]),
                reads=("sb:g_ones_col",), writes=("ones_b",))
            st("vector", lambda e: e.tensor_copy(coresb[:, :], coresf_sb[:, :]),
                reads=("sb:coresf_sb",), writes=("coresb",))
            for hf in range(2):
                st("sync", lambda e, hf=hf: e.dma_start(
                    out=wk_f[0][0:16, :],
                    in_=bw_t.ap()[:, hf * 2048:(hf + 1) * 2048]),
                    reads=("wkf0",), writes=("wkf0",), kind="dma", ch="wk0")
                st("vector", lambda e, hf=hf: e.tensor_copy(
                    bwb[:, hf * 2048:(hf + 1) * 2048], wk_f[0][0:16, :]),
                    reads=("wkf0",), writes=("bwb",))

            def stats_and_norm(vsrc, dst_fc, th, out_f32, okeys):
                for ts_ in range(2):
                    tkey = f"st{th}{ts_}"
                    for fc in range(32):
                        sq = sqb[fc % 2]
                        st("scalar", lambda e, fc=fc, ts_=ts_, sq=sq: e.activation(
                            sq[:, :], vsrc(fc, ts_), AF.Square, bias=0.0,
                            scale=1.0),
                            reads=okeys, writes=(f"sqb{fc % 2}",))
                        st("tensor", lambda e, fc=fc, ts_=ts_: e.matmul(
                            pmm[0][0:1, 0:512], ones_b[:, :], vsrc(fc, ts_),
                            start=(fc == 0), stop=(fc == 31)),
                            reads=okeys + ("ones_b",), writes=("pmm0",))
                        st("tensor", lambda e, fc=fc, sq=sq: e.matmul(
                            pmm[0][32:33, 512:1024], ones_b[:, :], sq[:, :],
                            start=(fc == 0), stop=(fc == 31)),
                            reads=(f"sqb{fc % 2}", "ones_b"), writes=("pmm0",))
                    st("vector", lambda e: e.tensor_scalar(
                        m_row[:, :], pmm[0][0:1, 0:512], 1.0 / DIM, None,
                        ALU.mult),
                        reads=("pmm0",), writes=("m_row",))
                    st("vector", lambda e: e.tensor_scalar(
                        v_row[:, :], pmm[0][32:33, 512:1024], 1.0 / DIM, None,
                        ALU.mult),
                        reads=("pmm0",), writes=("v_row",))
                    st("scalar", lambda e: e.activation(
                        t_row[:, :], m_row[:, :], AF.Square, bias=0.0,
                        scale=1.0),
                        reads=("m_row",), writes=("t_row",))
                    st("vector", lambda e: e.tensor_sub(
                        v_row[:, :], v_row[:, :], t_row[:, :]),
                        reads=("v_row", "t_row"), writes=("v_row",))
                    st("vector", lambda e: e.tensor_scalar(
                        v_row[:, :], v_row[:, :], EPS, None, ALU.add),
                        reads=("v_row",), writes=("v_row",))
                    st("scalar", lambda e: e.activation(
                        v_row[:, :], v_row[:, :], AF.Sqrt, bias=0.0,
                        scale=1.0),
                        reads=("v_row",), writes=("v_row",))
                    st("vector", lambda e: e.reciprocal(
                        t_row[:, :], v_row[:, :]),
                        reads=("v_row",), writes=("t_row",))
                    st("tensor", lambda e: e.matmul(
                        pmm[1][:, 0:512], g_triu0[0:1, :], m_row[:, :],
                        start=True, stop=True),
                        reads=("m_row", "sb:g_triu0"), writes=("pmm1",))
                    st("tensor", lambda e: e.matmul(
                        pmm[1][:, 512:1024], g_triu0[0:1, :], t_row[:, :],
                        start=True, stop=True),
                        reads=("t_row", "sb:g_triu0"), writes=("pmm1",))
                    for fc in range(32):
                        dst_fc(fc, ts_)

            for th in range(2):
                tsl = slice(th * 1024, (th + 1) * 1024)
                for og in range(8):
                    for kg in range(16):
                        sl = kg % 2
                        st("sync", lambda e, og=og, kg=kg, sl=sl: e.dma_start(
                            out=wk_f[sl][:, :],
                            in_=wout_g3.ap()[og:og + 1, kg:kg + 1, :, :]
                            .squeeze(0).squeeze(0)),
                            reads=(), writes=(f"wkf{sl}",), kind="dma",
                            ch=f"wk{sl}", label=f"wo{th}_{og}_{kg}")
                        ceng = "gpsimd" if kg % 2 == 0 else "vector"
                        st(ceng, lambda e, sl=sl: e.tensor_copy(
                            wk_b[sl][:, :], wk_f[sl][:, :]),
                            reads=(f"wkf{sl}",), writes=(f"wkb{sl}",))
                        ykeys = tuple(f"y2T:{kc}:{th}" for kc in
                                      range(kg * 4, kg * 4 + 4))
                        st("scalar", lambda e, kg=kg, th=th, sl=sl: e.dma_start(
                            out=yr[sl][:, :].rearrange(
                                "p (j t) -> p j t", j=4),
                            in_=y2T_d.ap()[kg * 512:(kg + 1) * 512,
                                           th * 1024:(th + 1) * 1024]
                            .rearrange("(j p) t -> p j t", j=4)),
                            reads=ykeys, writes=(f"yr{sl}",), kind="dma",
                            ch=f"ys{sl}", label=f"ys{th}_{og}_{kg}")
                        for j in range(4):
                            for i in range(4):
                                for ts_ in range(2):
                                    st("tensor", lambda e, i=i, j=j, ts_=ts_,
                                       sl=sl, kg=kg: e.matmul(
                                        pmm[i][:, ts_ * 512:(ts_ + 1) * 512],
                                        wk_b[sl][:, (j * 4 + i) * 128:
                                                 (j * 4 + i + 1) * 128],
                                        yr[sl][:, j * 1024 + ts_ * 512:
                                               j * 1024 + (ts_ + 1) * 512],
                                        start=(kg == 0 and j == 0),
                                        stop=(kg == 15 and j == 3)),
                                        reads=(f"wkb{sl}", f"yr{sl}"),
                                        writes=(f"pmm{i}",))
                    for i in range(4):
                        o_ = og * 4 + i
                        st("scalar", lambda e, i=i, o_=o_: e.activation(
                            obuf[:, o_ * 1024:(o_ + 1) * 1024], pmm[i][:, :],
                            AF.Identity, bias=0.0, scale=1.0),
                            reads=(f"pmm{i}",), writes=(f"ob:{o_}",))

                for fc in range(32):
                    for ts_ in range(2):
                        xs_ = x0s[ts_ % 2]
                        st("sync", lambda e, fc=fc, ts_=ts_, th=th, xs_=xs_:
                            e.dma_start(
                                out=xs_[:, :],
                                in_=x_inT.ap()[fc * 128:(fc + 1) * 128,
                                               th * 1024 + ts_ * 512:
                                               th * 1024 + (ts_ + 1) * 512]),
                            reads=(), writes=(f"x0s{ts_ % 2}",), kind="dma",
                            ch=f"x0c{ts_ % 2}")
                        st("vector", lambda e, fc=fc, ts_=ts_, xs_=xs_:
                            e.tensor_add(
                                obuf[:, fc * 1024 + ts_ * 512:
                                     fc * 1024 + (ts_ + 1) * 512],
                                obuf[:, fc * 1024 + ts_ * 512:
                                     fc * 1024 + (ts_ + 1) * 512],
                                xs_[:, :]),
                            reads=(f"ob:{fc}", f"x0s{ts_ % 2}"),
                            writes=(f"ob:{fc}",))

                okeys1 = tuple(f"ob:{fc}" for fc in range(32))

                def vsrc1(fc, ts_):
                    return obuf[:, fc * 1024 + ts_ * 512:
                                fc * 1024 + (ts_ + 1) * 512]

                def dst1(fc, ts_):
                    st("vector", lambda e, fc=fc, ts_=ts_: e.tensor_sub(
                        x1buf[:, fc * 1024 + ts_ * 512:
                              fc * 1024 + (ts_ + 1) * 512],
                        vsrc1(fc, ts_), pmm[1][:, 0:512]),
                        reads=okeys1 + ("pmm1",), writes=(f"x1:{fc}",))
                    st("vector", lambda e, fc=fc, ts_=ts_: e.tensor_mul(
                        x1buf[:, fc * 1024 + ts_ * 512:
                              fc * 1024 + (ts_ + 1) * 512],
                        x1buf[:, fc * 1024 + ts_ * 512:
                              fc * 1024 + (ts_ + 1) * 512],
                        pmm[1][:, 512:1024]),
                        reads=(f"x1:{fc}", "pmm1"), writes=(f"x1:{fc}",))

                stats_and_norm(vsrc1, dst1, th, False, okeys1)
                x1keys = tuple(f"x1:{fc}" for fc in range(32))

                st("sync", lambda e: e.dma_start(
                    out=aws, in_=aw_kt.ap()),
                    reads=("wkf0",), writes=("wkf0",), kind="dma",
                    ch="wk0")
                st("sync", lambda e: e.dma_start(
                    out=rws, in_=rw_kt.ap()),
                    reads=("wkf0",), writes=("wkf0",), kind="dma",
                    ch="wk0")
                st("vector", lambda e: e.tensor_copy(awb[:, :], aws),
                    reads=("wkf0",), writes=("awb",))
                st("vector", lambda e: e.tensor_copy(rwb[:, :], rws),
                    reads=("wkf0",), writes=("rwb",))
                for kc in range(32):
                    for ts_ in range(2):
                        st("tensor", lambda e, kc=kc, ts_=ts_: e.matmul(
                            pmm[2][0:16, ts_ * 512:(ts_ + 1) * 512],
                            awb[:, kc * 16:(kc + 1) * 16],
                            x1buf[:, kc * 1024 + ts_ * 512:
                                  kc * 1024 + (ts_ + 1) * 512],
                            start=(kc == 0), stop=(kc == 31)),
                            reads=("awb",) + x1keys, writes=("pmm2",))
                        st("tensor", lambda e, kc=kc, ts_=ts_: e.matmul(
                            pmm[3][0:8, ts_ * 512:(ts_ + 1) * 512],
                            rwb[:, kc * 8:(kc + 1) * 8],
                            x1buf[:, kc * 1024 + ts_ * 512:
                                  kc * 1024 + (ts_ + 1) * 512],
                            start=(kc == 0), stop=(kc == 31)),
                            reads=("rwb",) + x1keys, writes=("pmm3",))
                st("vector", lambda e: e.tensor_copy(
                    axf[:, :], pmm[2][0:16, :]),
                    reads=("pmm2",), writes=("axf",))
                st("vector", lambda e: e.tensor_copy(
                    rtf[:, :], pmm[3][0:8, :]),
                    reads=("pmm3",), writes=("rtf",))

                for i in range(8):
                    st("tensor", lambda e, i=i: e.transpose(
                        pmm[3][:, i * 8:(i + 1) * 8], rtf[:, i * 128:(i + 1) * 128],
                        g_ident_f[32:40, 32:40]),
                        reads=("rtf", "sb:g_ident_f"), writes=("pmm3",))
                    st("tensor", lambda e, i=i: e.transpose(
                        pmm[3][:, 64 + i * 16:64 + (i + 1) * 16],
                        axf[:, i * 128:(i + 1) * 128],
                        g_ident_f[0:16, 0:16]),
                        reads=("axf", "sb:g_ident_f"), writes=("pmm3",))
                for i in range(8):
                    st("vector", lambda e, i=i: e.reduce_max(
                        pcol[:, :], pmm[3][:, i * 8:(i + 1) * 8], axis=AX.X,
                        negate=True),
                        reads=("pmm3",), writes=("pcol",))
                    st("scalar", lambda e, i=i: e.activation(
                        ptok[:, i * 8:(i + 1) * 8], pmm[3][:, i * 8:(i + 1) * 8],
                        AF.Exp, bias=pcol[:, :], scale=1.0),
                        reads=("pmm3", "pcol"), writes=(f"pt:{i}",))
                    st("vector", lambda e, i=i: e.reduce_sum(
                        pcol[:, :], ptok[:, i * 8:(i + 1) * 8], axis=AX.X),
                        reads=(f"pt:{i}",), writes=("pcol",))
                    st("vector", lambda e, i=i: e.reciprocal(
                        pcol[:, :], pcol[:, :]),
                        reads=("pcol",), writes=("pcol",))
                    st("vector", lambda e, i=i: e.tensor_scalar(
                        ptok[:, i * 8:(i + 1) * 8], ptok[:, i * 8:(i + 1) * 8],
                        pcol[:, :], None, ALU.mult),
                        reads=(f"pt:{i}", "pcol"), writes=(f"pt:{i}",))
                    st("vector", lambda e, i=i: e.tensor_copy(
                        atok[:, 0:16], pmm[3][:, 64 + i * 16:64 + (i + 1) * 16]),
                        reads=("pmm3",), writes=("atok",))
                    st("vector", lambda e, i=i: e.tensor_tensor(
                        w2b[:, :].rearrange("p (a b) -> p a b", a=8),
                        ptok[:, i * 8:(i + 1) * 8][:, :, None]
                        .broadcast_to((128, 8, 16)),
                        atok[:, 0:16][:, None, :].broadcast_to((128, 8, 16)),
                        ALU.mult),
                        reads=(f"pt:{i}", "atok"), writes=("w2b",))
                    st("tensor", lambda e, i=i: e.transpose(
                        pmm[3][:, 192:320], w2b[:, :], g_ident_f[:, :]),
                        reads=("w2b", "sb:g_ident_f"), writes=("pmm3",))
                    st("vector", lambda e, i=i: e.tensor_copy(
                        w2ts[:, :], pmm[3][:, 192:320]),
                        reads=("pmm3",), writes=("w2ts",))
                    st("tensor", lambda e, i=i: e.matmul(
                        pmm[2][0:16, i * 128:(i + 1) * 128], coresb[:, :],
                        w2ts[:, :], start=True, stop=True),
                        reads=("coresb", "w2ts"), writes=("pmm2",))
                st("scalar", lambda e: e.activation(
                    mixb[:, :], pmm[2][0:16, :], AF.Identity, bias=0.0,
                    scale=SCALING),
                    reads=("pmm2",), writes=("mixb",))

                for og in range(8):
                    for kg in range(8):
                        sl = kg % 2
                        st("sync", lambda e, og=og, kg=kg, sl=sl: e.dma_start(
                            out=wk_f[sl][:, :],
                            in_=wbase_g3.ap()[og:og + 1, kg:kg + 1, :, :]
                            .squeeze(0).squeeze(0)),
                            reads=(), writes=(f"wkf{sl}",), kind="dma",
                            ch=f"wk{sl}", label=f"wb{th}_{og}_{kg}")
                        ceng = "gpsimd" if kg % 2 == 0 else "vector"
                        st(ceng, lambda e, sl=sl: e.tensor_copy(
                            wk_b[sl][:, :], wk_f[sl][:, :]),
                            reads=(f"wkf{sl}",), writes=(f"wkb{sl}",))
                        for j in range(4):
                            kc = kg * 4 + j
                            for i in range(4):
                                for ts_ in range(2):
                                    st("tensor", lambda e, i=i, j=j, ts_=ts_,
                                       sl=sl, kc=kc, kg=kg: e.matmul(
                                        pmm[i][:, ts_ * 512:(ts_ + 1) * 512],
                                        wk_b[sl][:, (j * 4 + i) * 128:
                                                 (j * 4 + i + 1) * 128],
                                        x1buf[:, kc * 1024 + ts_ * 512:
                                              kc * 1024 + (ts_ + 1) * 512],
                                        start=(kg == 0 and j == 0),
                                        stop=False),
                                        reads=(f"wkb{sl}",) + x1keys,
                                        writes=(f"pmm{i}",))
                    for i in range(4):
                        o_ = og * 4 + i
                        for ts_ in range(2):
                            st("tensor", lambda e, i=i, o_=o_, ts_=ts_: e.matmul(
                                pmm[i][:, ts_ * 512:(ts_ + 1) * 512],
                                bwb[0:16, o_ * 128:(o_ + 1) * 128],
                                mixb[:, ts_ * 512:(ts_ + 1) * 512],
                                start=False, stop=True),
                                reads=("bwb", "mixb"), writes=(f"pmm{i}",))
                        st("scalar", lambda e, i=i, o_=o_: e.activation(
                            obuf[:, o_ * 1024:(o_ + 1) * 1024], pmm[i][:, :],
                            AF.Identity, bias=g_bbase[:, o_:o_ + 1],
                            scale=1.0),
                            reads=(f"pmm{i}", "sb:g_bbase"),
                            writes=(f"ob:{o_}",))

                for fc in range(32):
                    st("gpsimd", lambda e, fc=fc: e.tensor_add(
                        obuf[:, fc * 1024:(fc + 1) * 1024],
                        obuf[:, fc * 1024:(fc + 1) * 1024],
                        x1buf[:, fc * 1024:(fc + 1) * 1024]),
                        reads=(f"ob:{fc}", f"x1:{fc}"), writes=(f"ob:{fc}",))

                def dst2(fc, ts_, th=th):
                    ob_ = outs[ts_ % 2]
                    st("vector", lambda e, fc=fc, ts_=ts_, ob_=ob_: e.tensor_sub(
                        ob_[:, :], vsrc1(fc, ts_), pmm[1][:, 0:512]),
                        reads=(f"ob:{fc}", "pmm1"), writes=(f"outs{ts_ % 2}",))
                    st("vector", lambda e, fc=fc, ts_=ts_, ob_=ob_: e.tensor_mul(
                        ob_[:, :], ob_[:, :], pmm[1][:, 512:1024]),
                        reads=(f"outs{ts_ % 2}", "pmm1"),
                        writes=(f"outs{ts_ % 2}",))
                    st("sync", lambda e, fc=fc, ts_=ts_, th=th, ob_=ob_:
                        e.dma_start(
                            out=out_t.ap()[fc * 128:(fc + 1) * 128,
                                           th * 1024 + ts_ * 512:
                                           th * 1024 + (ts_ + 1) * 512],
                            in_=ob_[:, :]),
                        reads=(f"outs{ts_ % 2}",),
                        writes=(f"out:{th}:{fc}:{ts_}",), kind="dma",
                        ch=f"sout{ts_ % 2}")

                stats_and_norm(vsrc1, dst2, th, True,
                               tuple(f"ob:{fc}" for fc in range(32)))

        p.fence()

        p.fence()
        dbg_map = dict(gateT=gateT, xbcT=xbcT, dtT_d=dtT_d, y1_tok=y1_tok,
                       ccA_out=ccA_out, sloc_d=sloc_d, ccB_out=ccB_out)
        for name in debug:
            src = dbg_map[name]
            sh = list(src.ap().shape)
            o = nc.dram_tensor("dbg_" + name, sh, src.ap().dtype,
                               kind="ExternalOutput")
            nrow, ncol = sh
            bb = ctx.enter_context(nc.sbuf_tensor(
                "dbgb_" + name, [128, min(ncol, 2048)], src.ap().dtype))
            bar = ()
            for r0 in range(0, nrow, 128):
                rr = min(128, nrow - r0)
                for c0 in range(0, ncol, 2048):
                    cw = min(2048, ncol - c0)
                    p.step("sync", lambda e, r0=r0, c0=c0, rr=rr, cw=cw,
                           src=src, bb=bb: e.dma_start(
                        out=bb[0:rr, 0:cw], in_=src.ap()[r0:r0 + rr, c0:c0 + cw]),
                        reads=bar, writes=("dbgb:" + name,), kind="dma",
                        ch="dbg_" + name)
                    p.step("sync", lambda e, r0=r0, c0=c0, rr=rr, cw=cw,
                           o=o, bb=bb: e.dma_start(
                        out=o.ap()[r0:r0 + rr, c0:c0 + cw], in_=bb[0:rr, 0:cw]),
                        reads=("dbgb:" + name,), writes=("dbgo:" + name,),
                        kind="dma", ch="dbg_" + name)

        p.fence()
        p.step("sync", lambda e: e.nop(), reads=(), writes=())

        sems = {}
        for name in p.sem_names():
            sems[name] = ctx.enter_context(nc.semaphore(name))
        p.emit(nc, block, sems)
    return nc, din



def _prep_shared(L0_attn_in_proj_w, L0_attn_conv_w, L0_attn_conv_b,
                 L0_attn_dt_bias, L0_attn_A_log, L0_attn_D, L0_attn_norm_w,
                 L0_attn_out_proj_w, L0_ffn_W_base, L0_ffn_b_base, L0_ffn_A_w,
                 L0_ffn_B_w, L0_ffn_router_w, L0_ffn_cores):
    f = np.float32
    sh = {}
    w = np.ascontiguousarray(L0_attn_in_proj_w, f)
    sh["win_t"] = np.ascontiguousarray(
        w.reshape(145, 128, 32, 128).transpose(0, 3, 2, 1))
    wo = np.ascontiguousarray(L0_attn_out_proj_w, f)
    sh["wout_g3"] = np.ascontiguousarray(
        wo.reshape(8, 4, 128, 16, 4, 128).transpose(0, 3, 5, 4, 1, 2)
        .reshape(8, 16, 128, 2048))
    wb = np.ascontiguousarray(L0_ffn_W_base, f)
    sh["wbase_g3"] = np.ascontiguousarray(
        wb.reshape(8, 4, 128, 8, 4, 128).transpose(0, 3, 5, 4, 1, 2)
        .reshape(8, 8, 128, 2048))
    aw = np.ascontiguousarray(L0_ffn_A_w, f)
    sh["aw_kt"] = np.ascontiguousarray(
        aw.T.reshape(32, 128, 16).transpose(1, 0, 2).reshape(128, 512))
    rw = np.ascontiguousarray(L0_ffn_router_w, f)
    sh["rw_kt"] = np.ascontiguousarray(
        rw.T.reshape(32, 128, 8).transpose(1, 0, 2).reshape(128, 256))
    sh["bw_t"] = np.ascontiguousarray(L0_ffn_B_w.T, f)
    sh["coresf"] = np.ascontiguousarray(
        np.asarray(L0_ffn_cores, f).reshape(128, 16))
    cw = np.asarray(L0_attn_conv_w, f).reshape(CONV_DIM, 4)
    sh["conv_w_r"] = np.ascontiguousarray(
        cw.reshape(80, 128, 4).transpose(1, 0, 2).reshape(128, 320))
    cb = np.asarray(L0_attn_conv_b, f)
    sh["conv_b_r"] = np.ascontiguousarray(cb.reshape(80, 128).T)
    sh["dt_bias_c"] = np.asarray(L0_attn_dt_bias, f).reshape(128, 1)
    sh["a_log_c"] = np.asarray(L0_attn_A_log, f).reshape(128, 1)
    sh["d_exp_row"] = np.ascontiguousarray(
        np.repeat(np.asarray(L0_attn_D, f), P).reshape(1, INTER))
    sh["d_row"] = np.asarray(L0_attn_D, f).reshape(1, 128)
    sh["normw_row"] = np.asarray(L0_attn_norm_w, f).reshape(1, INTER)
    sh["bbase_cols"] = np.ascontiguousarray(
        np.asarray(L0_ffn_b_base, f).reshape(32, 128).T)
    sh["ident_f"] = np.eye(128, dtype=f)
    sh["trilm1_f"] = np.tril(np.ones((128, 128), f), -1)
    sh["triu0_f"] = np.triu(np.ones((128, 128), f), 0)
    sl = np.zeros((128, 128), f)
    sl[127, :] = 1.0
    sh["sel_last_f"] = sl
    sh["ones_col"] = np.ones((128, 1), f)
    return sh


def _prep_core(core, hidden):
    f = np.float32
    b = core // 4
    pos = core % 4
    s0 = pos * T
    d = {}
    d["x_inT"] = np.ascontiguousarray(hidden[b, s0:s0 + T, :].T, f)
    pc = np.zeros((128, 4), f)
    if pos > 0:
        pc[:, pos - 1] = 1.0
    d["prev_cols"] = pc
    im = np.zeros((128, 4), f)
    im[:, :pos] = 1.0
    d["inmask_cols"] = im
    btw = np.zeros((1, 16), f)
    for j in range(4):
        for i in range(4):
            if j < i < pos:
                btw[0, j * 4 + i] = 1.0
    d["btw_row"] = btw
    return d


def kernel(**inputs):
    nc, _ = build()
    sh = _prep_shared(**{k: v for k, v in inputs.items()
                         if k != "L0_attn_hidden_states"})
    hidden = np.asarray(inputs["L0_attn_hidden_states"], np.float32)
    in_maps = []
    for c in range(NCORES):
        m = dict(sh)
        m.update(_prep_core(c, hidden))
        in_maps.append(m)
    res = run_bass_kernel_spmd(nc, in_maps, core_ids=list(range(NCORES)),
                               trace_cores=[0, 1, 2, 3])
    out = np.empty((2, 8192, DIM), np.float32)
    for c in range(NCORES):
        b, pos = c // 4, c % 4
        out[b, pos * T:(pos + 1) * T, :] = res.results[c]["out_t"].T
    return out


# version 10 (final)
# speedup vs baseline: 1.3498x; 1.3498x over previous
import numpy as np
import concourse.bass as bass
import concourse.mybir as mybir
from concourse.bass_utils import run_bass_kernel_spmd

F32 = mybir.dt.float32
BF16 = mybir.dt.bfloat16
AF = mybir.ActivationFunctionType
ALU = mybir.AluOpType
AX = mybir.AxisListType

DIM = 4096
H = 128
P = 64
G = 8
N = 128
CS = 128
INTER = 8192
CONV_DIM = 10240
NPROJ = 18560
T = 2048
NCH = 16
NCORES = 8
EPS = 1e-5
R = 16
E = 8
SCALING = 2.0

ENGINES = ("tensor", "vector", "scalar", "gpsimd", "sync")


class Prog:

    def __init__(self):
        self.steps = []
        self.last_writer = {}
        self.readers = {}
        self.fence_deps = ()

    def step(self, engine, fn, reads=(), writes=(), kind="c", ch=None,
             label=""):
        if kind == "dma":
            assert ch is not None, label
        deps = set(self.fence_deps)
        for r in reads:
            if r in self.last_writer:
                deps.add(self.last_writer[r])
        for w in writes:
            if w in self.last_writer:
                deps.add(self.last_writer[w])
            for r in self.readers.get(w, ()):
                deps.add(r)
        i = len(self.steps)
        self.steps.append(dict(engine=engine, fn=fn, kind=kind, deps=deps,
                               label=label, idx=i, ch=ch))
        for r in reads:
            self.readers.setdefault(r, []).append(i)
        for w in writes:
            self.last_writer[w] = i
            self.readers[w] = []
        return i

    def fence(self):
        last = {}
        for s in self.steps:
            last[s["engine"], s["kind"] == "dma" and s["ch"]] = s["idx"]
        self.fence_deps = tuple(last.values())

    def sem_names(self):
        names = set()
        for s in self.steps:
            if s["kind"] == "dma":
                names.add("d_" + s["ch"])
            elif s["kind"] == "cc":
                names.add(f"cc{sum(1 for t in self.steps[:s['idx']] if t['kind'] == 'cc')}")
            else:
                names.add("c_" + s["engine"])
        return sorted(names)

    def emit(self, nc, block, sems):
        c_idx = {e: 0 for e in ENGINES}
        d_idx = {}
        cc_idx = 0
        for s in self.steps:
            e = s["engine"]
            if s["kind"] == "dma":
                k = s["ch"]
                s["seq"] = d_idx.get(k, 0)
                d_idx[k] = s["seq"] + 1
                s["sem"] = "d_" + k
                s["val"] = 16 * (s["seq"] + 1)
            elif s["kind"] == "cc":
                s["seq"] = cc_idx
                s["sem"] = f"cc{cc_idx}"
                s["val"] = 1
                cc_idx += 1
            else:
                s["seq"] = c_idx[e]
                c_idx[e] += 1
                s["sem"] = "c_" + e
                s["val"] = s["seq"] + 1
        for e in ENGINES:
            assert c_idx[e] < 60000, (e, c_idx[e])
        for k, v in d_idx.items():
            assert 16 * v < 60000, (k, v)

        clocks = [None] * len(self.steps)
        eng_clock = {e: {} for e in ENGINES}
        per_engine = {e: [] for e in ENGINES}
        for s in self.steps:
            e = s["engine"]
            need = {}
            for d in s["deps"]:
                for k, v in clocks[d].items():
                    if need.get(k, 0) < v:
                        need[k] = v
            if s["kind"] == "dma" and s["seq"] > 0:
                assert need.get(s["sem"], 0) >= 16 * s["seq"], (
                    "channel not serialized", s["label"], s["sem"], s["seq"],
                    need.get(s["sem"], 0))
            waits = []
            cur = eng_clock[e]
            for k, v in sorted(need.items()):
                if k == "c_" + e and s["kind"] != "dma" and e == "tensor":
                    continue
                if cur.get(k, 0) < v:
                    waits.append((k, v))
                    cur[k] = v
            for k, v in need.items():
                if cur.get(k, 0) < v:
                    cur[k] = v
            s["waits"] = waits
            myc = dict(cur)
            if myc.get(s["sem"], 0) < s["val"]:
                myc[s["sem"]] = s["val"]
            clocks[s["idx"]] = myc
            per_engine[e].append(s)

        for e in ENGINES:
            steps = per_engine[e]
            if not steps:
                continue

            def make_body(steps=steps):
                def body(eng):
                    for s in steps:
                        for k, v in s["waits"]:
                            eng.wait_ge(sems[k], v)
                        ins = s["fn"](eng)
                        ins.then_inc(sems[s["sem"]], 16 if s["kind"] == "dma"
                                     else 1)
                return body

            getattr(block, e)(make_body())


def build(debug=(), feed=(), stop_after=None):
    import os
    skip_cc = bool(int(os.environ.get("K_SKIP_CC", "0")))
    nc = bass.Bass("TRN2", target_bir_lowering=False)
    p = Prog()

    din = {}

    def inp(name, shape, dtype=F32):
        din[name] = nc.dram_tensor(name, list(shape), dtype, kind="ExternalInput")
        return din[name]

    x_inT = inp("x_inT", [DIM, T])
    win_t = inp("win_t", [145, 128, 32, 128])
    wout_g3 = inp("wout_g3", [8, 16, 128, 2048])
    wbase_g3 = inp("wbase_g3", [8, 8, 128, 2048])
    aw_kt = inp("aw_kt", [128, 512])
    rw_kt = inp("rw_kt", [128, 256])
    bw_t = inp("bw_t", [16, 4096])
    coresf = inp("coresf", [128, 16])
    conv_w_r = inp("conv_w_r", [128, 320])
    conv_b_r = inp("conv_b_r", [128, 80])
    dt_bias_c = inp("dt_bias_c", [128, 1])
    a_log_c = inp("a_log_c", [128, 1])
    d_exp_row = inp("d_exp_row", [1, INTER])
    d_row = inp("d_row", [1, 128])
    normw_row = inp("normw_row", [1, INTER])
    normw_cols = inp("normw_cols", [128, 64])
    bbase_cols = inp("bbase_cols", [128, 32])
    ident_f = inp("ident_f", [128, 128])
    trilm1_f = inp("trilm1_f", [128, 128])
    triu0_f = inp("triu0_f", [128, 128])
    sel_last_f = inp("sel_last_f", [128, 128])
    ones_col = inp("ones_col", [128, 1])
    prev_cols = inp("prev_cols", [128, 4])
    inmask_cols = inp("inmask_cols", [128, 4])
    btw_row = inp("btw_row", [1, 16])

    out_t = nc.dram_tensor("out_t", [DIM, T], F32, kind="ExternalOutput")
    k1 = "ExternalInput" if "p1" in feed else "Internal"
    gateT = nc.dram_tensor("gateT", [INTER, T], BF16, kind=k1)
    xbcT = nc.dram_tensor("xbcT", [CONV_DIM, T], BF16, kind=k1)
    dtT_d = nc.dram_tensor("dtT_d", [128, T], F32, kind=k1)
    y1_tok = nc.dram_tensor("y1_tok", [T, INTER], BF16)
    y2T_d = nc.dram_tensor("y2T_d", [INTER, T], BF16)
    sloc_d = nc.dram_tensor("sloc_d", [128, INTER], F32)
    ccA_in = nc.dram_tensor("ccA_in", [128, 240], F32)
    ccA_out = nc.dram_tensor("ccA_out", [512, 240], F32)
    ccB_in = nc.dram_tensor("ccB_in", [128, 8193], F32)
    ccB_out = nc.dram_tensor("ccB_out", [512, 8193], F32)
    from contextlib import ExitStack
    ctx = ExitStack()
    with ctx:
        block = ctx.enter_context(nc.Block())

        SB = lambda n, sh, dt=F32: ctx.enter_context(nc.sbuf_tensor(n, list(sh), dt))
        PS = lambda n, sh, dt=F32: ctx.enter_context(nc.psum_tensor(n, list(sh), dt))

        g_ident_b = SB("g_ident_b", [128, 128], BF16)
        g_ident_f = SB("g_ident_f", [128, 128], F32)
        g_trilm1 = SB("g_trilm1", [128, 128], F32)
        g_triu0 = SB("g_triu0", [128, 128], F32)
        g_sel_last = SB("g_sel_last", [128, 128], F32)
        g_ones_col = SB("g_ones_col", [128, 1], F32)
        g_convw = SB("g_convw", [128, 320], F32)
        g_convb = SB("g_convb", [128, 80], F32)
        g_dtb = SB("g_dtb", [128, 1], F32)
        g_alog = SB("g_alog", [128, 1], F32)
        g_bbase = SB("g_bbase", [128, 32], F32)
        g_prev = SB("g_prev", [128, 4], F32)
        g_inmask = SB("g_inmask", [128, 4], F32)
        g_btw = SB("g_btw", [1, 16], F32)
        d_row_sb = SB("d_row_sb", [1, 128], F32)
        coresf_sb = SB("coresf_sb", [128, 16], F32)

        _ld_prev = [None]

        def ld(dst, src, eng="sync"):
            rd = (_ld_prev[0],) if _ld_prev[0] else ()
            key = "sb:" + dst.tensor.name
            p.step(eng, lambda e, d=dst, s=src: e.dma_start(out=d, in_=s),
                   reads=rd, writes=(key,), kind="dma", ch="ldg")
            _ld_prev[0] = key

        ld(g_ident_f[:, :], ident_f[:, :])
        ld(g_trilm1[:, :], trilm1_f[:, :])
        ld(g_triu0[:, :], triu0_f[:, :])
        ld(g_sel_last[:, :], sel_last_f[:, :])
        ld(g_ones_col[:, :], ones_col[:, :])
        ld(g_convw[:, :], conv_w_r[:, :])
        ld(g_convb[:, :], conv_b_r[:, :])
        ld(g_dtb[:, :], dt_bias_c[:, :])
        ld(g_alog[:, :], a_log_c[:, :])
        ld(g_bbase[:, :], bbase_cols[:, :])
        ld(g_prev[:, :], prev_cols[:, :])
        ld(g_inmask[:, :], inmask_cols[:, :])
        ld(g_btw[:, :], btw_row[:, :])
        ld(d_row_sb[:, :], d_row[:, :])
        ld(coresf_sb[:, :], coresf[:, :])
        p.step("vector", lambda e: e.tensor_copy(g_ident_b[:, :], g_ident_f[:, :]),
               reads=("sb:g_ident_f",), writes=("sb:g_ident_b",))

        ph1 = ExitStack()
        with ph1 if "p1" not in feed else ExitStack():
          if "p1" not in feed:
              xin_bf = ph1.enter_context(nc.sbuf_tensor("xin_bf", [128, 32 * T], BF16))
              xstg = [ph1.enter_context(nc.sbuf_tensor(f"xstg{i}", [128, T], F32))
                      for i in range(1)]
              wsl_f = [ph1.enter_context(nc.sbuf_tensor(f"wsl_f{i}", [128, 4096], F32))
                       for i in range(2)]
              wsl_b = [ph1.enter_context(nc.sbuf_tensor(f"wsl_b{i}", [128, 4096], BF16))
                       for i in range(2)]
              cstage = ph1.enter_context(nc.sbuf_tensor("cstage", [128, 515], F32))
              ctail = ph1.enter_context(nc.sbuf_tensor("ctail", [128, 3], F32))
              cacc = ph1.enter_context(nc.sbuf_tensor("cacc", [128, 512], F32))
              evb = [ph1.enter_context(nc.sbuf_tensor(f"evb{i}", [128, 512], BF16))
                     for i in range(4)]
              evf = [ph1.enter_context(nc.sbuf_tensor(f"evf{i}", [128, 512], F32))
                     for i in range(2)]
              cca_st = ph1.enter_context(nc.sbuf_tensor("cca_st", [128, 240], F32))
              own3 = ph1.enter_context(nc.sbuf_tensor("own3", [128, 240], F32))
              pp = [ph1.enter_context(nc.psum_tensor(f"pp{i}", [128, 2048], F32))
                    for i in range(2)]

              for kc in range(32):
                  st = xstg[0]
                  key = "xstg0"
                  p.step("sync", lambda e, st=st, kc=kc: e.dma_start(
                      out=st[:, :], in_=x_inT[kc * 128:(kc + 1) * 128, :]),
                      reads=(), writes=(key,), kind="dma", ch="xstg",
                      label=f"ldx{kc}")
                  eng = "scalar" if kc % 2 == 0 else "vector"
                  if eng == "scalar":
                      p.step(eng, lambda e, st=st, kc=kc: e.activation(
                          xin_bf[:, kc * T:(kc + 1) * T], st[:, :], AF.Identity,
                          bias=0.0, scale=1.0),
                          reads=(key,), writes=(f"xinbf:{kc}",))
                  else:
                      p.step(eng, lambda e, st=st, kc=kc: e.tensor_copy(
                          xin_bf[:, kc * T:(kc + 1) * T], st[:, :]),
                          reads=(key,), writes=(f"xinbf:{kc}",))

              xin_reads = tuple(f"xinbf:{kc}" for kc in range(32))

              def emit_wload(o_):
                  sl = o_ % 2
                  wkey = f"wsl{sl}"
                  wbkey = f"wslb{sl}"
                  p.step("sync", lambda e, o_=o_, sl=sl: e.dma_start(
                      out=wsl_f[sl][:, :].rearrange("p (kc o) -> p kc o", kc=32),
                      in_=win_t.ap()[o_:o_ + 1, :, :, :].squeeze(0)),
                      reads=(wkey,), writes=(wkey,), kind="dma", ch=f"wsl{sl}",
                      label=f"ldw{o_}")
                  for hf in range(2):
                      dst = wsl_b[sl][:, hf * 2048:(hf + 1) * 2048]
                      srcv = wsl_f[sl][:, hf * 2048:(hf + 1) * 2048]
                      p.step("gpsimd", lambda e, dst=dst, srcv=srcv:
                          e.tensor_copy(dst, srcv),
                          reads=(wkey,), writes=(wbkey,))

              emit_wload(0)
              for o_ in range(145):
                  sl = o_ % 2
                  pb = o_ % 2
                  wbkey = f"wslb{sl}"
                  if o_ + 1 < 145:
                      emit_wload(o_ + 1)

                  pkey = f"pp{pb}"
                  for kc in range(32):
                      for tt_ in range(4):
                          p.step("tensor", lambda e, pb=pb, sl=sl, kc=kc, tt_=tt_: e.matmul(
                              pp[pb][:, tt_ * 512:(tt_ + 1) * 512],
                              wsl_b[sl][:, kc * 128:(kc + 1) * 128],
                              xin_bf[:, kc * T + tt_ * 512: kc * T + (tt_ + 1) * 512],
                              start=(kc == 0), stop=(kc == 31)),
                              reads=(wbkey, f"xinbf:{kc}"), writes=(pkey,),
                              label=f"mm{o_}")

                  if o_ < 64:
                      for tt_ in range(4):
                          eb = evb[tt_ % 4]
                          ekey = f"evb{tt_ % 4}"
                          p.step("scalar", lambda e, pb=pb, tt_=tt_, eb=eb: e.activation(
                              eb[:, :], pp[pb][:, tt_ * 512:(tt_ + 1) * 512],
                              AF.Identity, bias=0.0, scale=1.0),
                              reads=(pkey,), writes=(ekey,))
                          p.step("scalar", lambda e, o_=o_, tt_=tt_, eb=eb: e.dma_start(
                              out=gateT.ap()[o_ * 128:(o_ + 1) * 128,
                                             tt_ * 512:(tt_ + 1) * 512],
                              in_=eb[:, :]),
                              reads=(ekey,), writes=(f"gateT:{o_}:{tt_}",),
                              kind="dma", ch=f"sevb{tt_ % 4}")
                  elif o_ < 144:
                      cc = o_ - 64
                      for tt_ in range(4):
                          p.step("vector", lambda e, pb=pb, tt_=tt_: e.tensor_copy(
                              cstage[:, 3:515], pp[pb][:, tt_ * 512:(tt_ + 1) * 512]),
                              reads=(pkey,), writes=("cstage",))
                          if tt_ == 0:
                              p.step("vector", lambda e: e.memset(cstage[:, 0:3], 0.0),
                                     reads=(), writes=("cstage",))
                              p.step("vector", lambda e, cc=cc: e.tensor_copy(
                                  own3[:, cc * 3:cc * 3 + 3], cstage[:, 3:6]),
                                  reads=("cstage",), writes=("own3",))
                          else:
                              p.step("vector", lambda e: e.tensor_copy(
                                  cstage[:, 0:3], ctail[:, :]),
                                  reads=("ctail",), writes=("cstage",))
                          if tt_ == 3:
                              p.step("vector", lambda e, cc=cc: e.tensor_copy(
                                  cca_st[:, cc * 3:cc * 3 + 3], cstage[:, 512:515]),
                                  reads=("cstage",), writes=("cca_st",))
                          else:
                              p.step("vector", lambda e: e.tensor_copy(
                                  ctail[:, :], cstage[:, 512:515]),
                                  reads=("cstage",), writes=("ctail",))
                          p.step("vector", lambda e, cc=cc: e.tensor_scalar(
                              cacc[:, :], cstage[:, 0:512],
                              g_convw[:, cc * 4:cc * 4 + 1], None, ALU.mult),
                              reads=("cstage", "sb:g_convw"), writes=("cacc",))
                          for j in range(1, 4):
                              eng = "vector"
                              p.step(eng, lambda e, cc=cc, j=j: e.scalar_tensor_tensor(
                                  cacc[:, :], cstage[:, j:j + 512],
                                  g_convw[:, cc * 4 + j:cc * 4 + j + 1],
                                  cacc[:, :], ALU.mult, ALU.add),
                                  reads=("cstage", "cacc", "sb:g_convw"),
                                  writes=("cacc",))
                          eb = evb[tt_ % 4]
                          ekey = f"evb{tt_ % 4}"
                          p.step("scalar", lambda e, cc=cc, eb=eb: e.activation(
                              eb[:, :], cacc[:, :], AF.Silu,
                              bias=g_convb[:, cc:cc + 1], scale=1.0),
                              reads=("cacc", "sb:g_convb"), writes=(ekey,))
                          p.step("scalar", lambda e, cc=cc, tt_=tt_, eb=eb: e.dma_start(
                              out=xbcT.ap()[cc * 128:(cc + 1) * 128,
                                            tt_ * 512:(tt_ + 1) * 512],
                              in_=eb[:, :]),
                              reads=(ekey,), writes=(f"xbcT:{cc}:{tt_}",),
                              kind="dma", ch=f"sevb{tt_ % 4}")
                  else:
                      for tt_ in range(4):
                          ef = evf[tt_ % 2]
                          ekey = f"evf{tt_ % 2}"
                          p.step("scalar", lambda e, pb=pb, tt_=tt_, ef=ef: e.activation(
                              ef[:, :], pp[pb][:, tt_ * 512:(tt_ + 1) * 512],
                              AF.Identity, bias=0.0, scale=1.0),
                              reads=(pkey,), writes=(ekey,))
                          p.step("scalar", lambda e, tt_=tt_, ef=ef: e.dma_start(
                              out=dtT_d.ap()[:, tt_ * 512:(tt_ + 1) * 512],
                              in_=ef[:, :]),
                              reads=(ekey,), writes=(f"dtT:{tt_}",), kind="dma",
                              ch=f"sevf{tt_ % 2}")

              p.step("sync", lambda e: e.dma_start(
                  out=ccA_in.ap(), in_=cca_st[:, :]),
                  reads=("cca_st",), writes=("dram:ccA_in",), kind="dma",
                  ch="ccA")

              if not skip_cc:
                  p.step("gpsimd", lambda e: e.collective_compute(
                      "AllGather", ALU.bypass,
                      replica_groups=[[0, 1, 2, 3], [4, 5, 6, 7]],
                      ins=[ccA_in.ap().opt()], outs=[ccA_out.ap().opt()]),
                      reads=("dram:ccA_in",),
                      writes=("dram:ccA_out",), kind="cc")

              gtail = ph1.enter_context(nc.sbuf_tensor("gtail", [128, 240], F32))
              gt_j = ph1.enter_context(nc.sbuf_tensor("gt_j", [128, 240], F32))
              p.step("vector", lambda e: e.memset(gtail[:, :], 0.0),
                     reads=(), writes=("gtail",))
              for j in range(4):
                  p.step("sync", lambda e, j=j: e.dma_start(
                      out=gt_j[:, :],
                      in_=ccA_out.ap()[j * 128:(j + 1) * 128, :]),
                      reads=("dram:ccA_out",), writes=("gt_j",), kind="dma",
                      ch="gtj")
                  p.step("vector", lambda e, j=j: e.scalar_tensor_tensor(
                      gtail[:, :], gt_j[:, :], g_prev[:, j:j + 1], gtail[:, :],
                      ALU.mult, ALU.add),
                      reads=("gt_j", "sb:g_prev", "gtail"), writes=("gtail",))

              p6 = ph1.enter_context(nc.sbuf_tensor("p6", [128, 6], F32))
              pacc3 = ph1.enter_context(nc.sbuf_tensor("pacc3", [128, 3], F32))
              pout3 = ph1.enter_context(nc.sbuf_tensor("pout3", [128, 3], BF16))
              for cc in range(80):
                  p.step("vector", lambda e, cc=cc: e.tensor_copy(
                      p6[:, 0:3], gtail[:, cc * 3:cc * 3 + 3]),
                      reads=("gtail",), writes=("p6",))
                  p.step("vector", lambda e, cc=cc: e.tensor_copy(
                      p6[:, 3:6], own3[:, cc * 3:cc * 3 + 3]),
                      reads=("own3",), writes=("p6",))
                  p.step("vector", lambda e, cc=cc: e.tensor_scalar(
                      pacc3[:, :], p6[:, 0:3],
                      g_convw[:, cc * 4:cc * 4 + 1], None, ALU.mult),
                      reads=("p6", "sb:g_convw"), writes=("pacc3",))
                  for j in range(1, 4):
                      p.step("vector", lambda e, cc=cc, j=j: e.scalar_tensor_tensor(
                          pacc3[:, :], p6[:, j:j + 3],
                          g_convw[:, cc * 4 + j:cc * 4 + j + 1],
                          pacc3[:, :], ALU.mult, ALU.add),
                          reads=("p6", "pacc3", "sb:g_convw"), writes=("pacc3",))
                  p.step("scalar", lambda e, cc=cc: e.activation(
                      pout3[:, :], pacc3[:, :], AF.Silu,
                      bias=g_convb[:, cc:cc + 1], scale=1.0),
                      reads=("pacc3", "sb:g_convb"), writes=("pout3",))
                  p.step("scalar", lambda e, cc=cc: e.dma_start(
                      out=xbcT.ap()[cc * 128:(cc + 1) * 128, 0:3],
                      in_=pout3[:, :]),
                      reads=("pout3",), writes=(f"xbcT:{cc}:0",), kind="dma",
                      ch="spatch")

        p.fence()

        ph2 = ExitStack()
        with ph2:
            S2 = lambda n, sh, dt=F32: ph2.enter_context(
                nc.sbuf_tensor(n, list(sh), dt))
            f_dt = S2("f_dt", [128, T])
            f_adt = S2("f_adt", [128, T])
            f_acum = S2("f_acum", [128, T])
            f_wst = S2("f_wst", [128, T])
            cmask = S2("cmask", [128, T])
            a_col = S2("a_col", [128, 1])
            aseg = S2("aseg", [128, 1])
            Pst = S2("P", [128, INTER])
            P_bf = S2("P_bf", [128, INTER], BF16)
            x_tokP = [S2(f"x_tok{i}", [128, INTER], BF16) for i in range(2)]
            x_stP = [S2(f"x_st{i}", [128, 1024], BF16) for i in range(2)]
            y_ch = S2("y_ch", [128, INTER], BF16)
            bcP = [S2(f"bc_sb{i}", [128, 2048], BF16) for i in range(2)]
            btokP = [S2(f"btok{i}", [128, 1024], BF16) for i in range(2)]
            gmaskP = [S2(f"gmask{i}", [128, 1024], BF16) for i in range(2)]
            lhsbP = [S2("lhsb", [128, 2048], F32), cmask]
            t2P = [S2(f"t2_{i}", [128, 2048], BF16) for i in range(2)]
            mbatP = [S2(f"mbat{i}", [128, 2048], BF16) for i in range(2)]
            tok_fP = [S2(f"tok_f{i}", [128, 512], F32) for i in range(2)]
            e_tokP = [S2(f"e_tok{i}", [128, 128], F32) for i in range(2)]
            drepP = [S2(f"drep_c{i}", [128, 128], F32) for i in range(2)]
            d_rep = S2("d_rep", [128, 128], F32)
            t_off = S2("t_off", [128, 1024], BF16)
            t_off2 = S2("t_off2", [128, 1024], BF16)
            ptmp = S2("ptmp", [128, 512], F32)
            x_tok = x_tokP[0]
            x_st = x_stP[0]
            bc_sb = bcP[0]
            tok_f = tok_fP[0]
            e_tok = e_tokP[0]
            drep_c = drepP[0]
            t2 = t2P[0]
            mbat = mbatP[0]
            pyD = ph2.enter_context(nc.psum_tensor("pyD", [128, 1024], F32))
            pyO = ph2.enter_context(nc.psum_tensor("pyO", [128, 1024], F32))
            pst = ph2.enter_context(nc.psum_tensor("pst", [128, 512], F32))
            pdiff = ph2.enter_context(nc.psum_tensor("pdiff", [128, 1024], F32))
            pwk_f = ph2.enter_context(nc.psum_tensor("pwk_f", [128, 512], F32))

            st = p.step

            for tt_ in range(4):
                st("sync", lambda e, tt_=tt_: e.dma_start(
                    out=f_dt[:, tt_ * 512:(tt_ + 1) * 512],
                    in_=dtT_d.ap()[:, tt_ * 512:(tt_ + 1) * 512]),
                    reads=(f"dtT:{tt_}",), writes=("f_dt",), kind="dma",
                    ch="ldt")
            st("vector", lambda e: e.tensor_scalar(
                f_dt[:, :], f_dt[:, :], g_dtb[:, :], None, ALU.add),
                reads=("f_dt", "sb:g_dtb"), writes=("f_dt",))
            st("scalar", lambda e: e.activation(
                f_adt[:, :], f_dt[:, :], AF.Abs, bias=0.0, scale=1.0),
                reads=("f_dt",), writes=("f_adt",))
            st("scalar", lambda e: e.activation(
                f_adt[:, :], f_adt[:, :], AF.Exp, bias=0.0, scale=-1.0),
                reads=("f_adt",), writes=("f_adt",))
            st("scalar", lambda e: e.activation(
                f_adt[:, :], f_adt[:, :], AF.Ln, bias=1.0, scale=1.0),
                reads=("f_adt",), writes=("f_adt",))
            st("vector", lambda e: e.tensor_scalar(
                f_wst[:, :], f_dt[:, :], 0.0, None, ALU.max),
                reads=("f_dt",), writes=("f_wst",))
            st("vector", lambda e: e.tensor_add(
                f_dt[:, :], f_wst[:, :], f_adt[:, :]),
                reads=("f_wst", "f_adt"), writes=("f_dt",))
            st("scalar", lambda e: e.activation(
                a_col[:, :], g_alog[:, :], AF.Exp, bias=0.0, scale=1.0),
                reads=("sb:g_alog",), writes=("a_col",))
            st("vector", lambda e: e.tensor_scalar(
                a_col[:, :], a_col[:, :], -1.0, None, ALU.mult),
                reads=("a_col",), writes=("a_col",))
            st("vector", lambda e: e.tensor_scalar(
                f_adt[:, :], f_dt[:, :], a_col[:, :], None, ALU.mult),
                reads=("f_dt", "a_col"), writes=("f_adt",))
            st("vector", lambda e: e.memset(cmask[:, :], 1.0),
                reads=(), writes=("cmask",))
            st("vector", lambda e: e.memset(cmask[:, 0:T:CS], 0.0),
                reads=(), writes=("cmask",))
            st("vector", lambda e: e.tensor_tensor_scan(
                f_acum[:, :], cmask[:, :], f_adt[:, :], 0.0,
                ALU.mult, ALU.add),
                reads=("cmask", "f_adt"), writes=("f_acum",))
            endv = f_acum[:, CS - 1:T:CS]
            st("vector", lambda e: e.tensor_tensor(
                f_wst[:, :].rearrange("p (c l) -> p c l", c=NCH),
                f_acum[:, :].rearrange("p (c l) -> p c l", c=NCH),
                endv[:, :, None].broadcast_to((128, NCH, CS)),
                ALU.subtract),
                reads=("f_acum",), writes=("f_wst",))
            st("scalar", lambda e: e.activation(
                f_wst[:, :], f_wst[:, :], AF.Exp, bias=0.0, scale=-1.0),
                reads=("f_wst",), writes=("f_wst",))
            st("vector", lambda e: e.tensor_mul(
                f_wst[:, :], f_wst[:, :], f_dt[:, :]),
                reads=("f_wst", "f_dt"), writes=("f_wst",))
            st("vector", lambda e: e.reduce_sum(
                aseg[:, :], f_acum[:, CS - 1:T:CS], axis=AX.X),
                reads=("f_acum",), writes=("aseg",))
            st("tensor", lambda e: e.matmul(
                pwk_f[:, 0:128], g_triu0[0:1, :], d_row_sb[:, :],
                start=True, stop=True),
                reads=("sb:g_triu0", "sb:d_row_sb"), writes=("pwk_f",))
            st("scalar", lambda e: e.activation(
                d_rep[:, :], pwk_f[:, 0:128], AF.Identity, bias=0.0,
                scale=1.0),
                reads=("pwk_f",), writes=("d_rep",))
            st("vector", lambda e: e.memset(Pst[:, :], 0.0),
                reads=(), writes=("P",))

            xre = lambda a: a.rearrange("p (h q) -> p h q", h=H)

            def emit_chunk_loads(c):
                cp = c % 2
                csl = slice(c * CS, (c + 1) * CS)
                tt_c = c // 4
                xkeys = tuple(f"xbcT:{cc}:{tt_c}" for cc in range(64))
                bckeys = tuple(f"xbcT:{cc}:{tt_c}" for cc in range(64, 80))
                if tt_c == 0:
                    xkeys += tuple(f"xbcT:{cc}:0" for cc in range(64))
                    bckeys += tuple(f"xbcT:{cc}:0" for cc in range(64, 80))
                st("sync", lambda e, c=c, cp=cp: e.dma_start_transpose(
                    x_tokP[cp][:, :], xbcT.ap()[0:INTER, c * CS:(c + 1) * CS]),
                    reads=xkeys, writes=(f"x_tok{cp}",), kind="dma",
                    ch=f"xtok{cp}", label=f"xtok{c}")
                st("scalar", lambda e, c=c, cp=cp: e.dma_start(
                    out=bcP[cp][:, 0:1024].rearrange("p (g l) -> p g l", g=8),
                    in_=xbcT.ap()[INTER:INTER + 1024, c * CS:(c + 1) * CS]
                    .rearrange("(g p) l -> p g l", g=8)),
                    reads=bckeys, writes=(f"bc_sb{cp}",), kind="dma",
                    ch=f"bcB{cp}", label=f"bcB{c}")
                st("scalar", lambda e, c=c, cp=cp: e.dma_start(
                    out=bcP[cp][:, 1024:2048].rearrange("p (g l) -> p g l", g=8),
                    in_=xbcT.ap()[INTER + 1024:INTER + 2048,
                                  c * CS:(c + 1) * CS]
                    .rearrange("(g p) l -> p g l", g=8)),
                    reads=bckeys, writes=(f"bc_sb{cp}",), kind="dma",
                    ch=f"bcC{cp}", label=f"bcC{c}")
                st("sync", lambda e, c=c, cp=cp: e.dma_start_transpose(
                    btokP[cp][:, :],
                    xbcT.ap()[INTER:INTER + 1024, c * CS:(c + 1) * CS]),
                    reads=bckeys, writes=(f"btok{cp}",), kind="dma",
                    ch=f"btk{cp}", label=f"btk{c}")

            def emit_fields(c):
                cp = c % 2
                csl = slice(c * CS, (c + 1) * CS)
                for i, (f, fk) in enumerate(((f_adt, "f_adt"), (f_dt, "f_dt"),
                                             (f_wst, "f_wst"),
                                             (f_acum, "f_acum"))):
                    st("tensor", lambda e, i=i, f=f, csl=csl: e.transpose(
                        pwk_f[:, i * 128:(i + 1) * 128], f[:, csl],
                        g_ident_f[:, :]),
                        reads=(fk, "sb:g_ident_f"), writes=("pwk_f",))
                st("vector", lambda e, cp=cp: e.tensor_copy(
                    tok_fP[cp][:, :], pwk_f[:, :]),
                    reads=("pwk_f",), writes=(f"tok_f{cp}",))
                st("scalar", lambda e, cp=cp: e.activation(
                    e_tokP[cp][:, :], tok_fP[cp][:, 384:512], AF.Exp,
                    bias=0.0, scale=1.0),
                    reads=(f"tok_f{cp}",), writes=(f"e_tok{cp}",))
                st("tensor", lambda e, cp=cp: e.matmul(
                    pwk_f[:, 0:128], g_sel_last[:, :], tok_fP[cp][:, 384:512],
                    start=True, stop=True),
                    reads=("sb:g_sel_last", f"tok_f{cp}"), writes=("pwk_f",))
                st("scalar", lambda e, cp=cp: e.activation(
                    drepP[cp][:, :], pwk_f[:, 0:128], AF.Exp, bias=0.0,
                    scale=1.0),
                    reads=("pwk_f",), writes=(f"drep_c{cp}",))
                for rnd in range(2):
                    for gg in range(4):
                        g = rnd * 4 + gg
                        st("tensor", lambda e, g=g, gg=gg, cp=cp: e.matmul(
                            pwk_f[:, gg * 128:(gg + 1) * 128],
                            bcP[cp][:, g * 128:(g + 1) * 128],
                            bcP[cp][:, 1024 + g * 128:1024 + (g + 1) * 128],
                            start=True, stop=True),
                            reads=(f"bc_sb{cp}",), writes=("pwk_f",))
                    st("vector", lambda e, rnd=rnd, cp=cp: e.tensor_tensor(
                        gmaskP[cp][:, rnd * 512:(rnd + 1) * 512]
                        .rearrange("p (g l) -> p g l", g=4),
                        pwk_f[:, :].rearrange("p (g l) -> p g l", g=4),
                        g_triu0[:, :, None].transpose([0, 2, 1])
                        .broadcast_to((128, 4, 128)),
                        ALU.mult),
                        reads=("pwk_f", "sb:g_triu0"), writes=(f"gmask{cp}",))

            def emit_lhsb(c, g):
                cp, gp = c % 2, g % 2
                for hh in range(2):
                    h0 = g * 16 + hh * 8
                    st("gpsimd", lambda e, h0=h0, hh=hh, gp=gp, cp=cp:
                        e.tensor_tensor(
                            lhsbP[gp][:, hh * 1024:(hh + 1) * 1024]
                            .rearrange("p (h s) -> p h s", h=8),
                            g_trilm1[:, None, :].broadcast_to((128, 8, 128)),
                            tok_fP[cp][:, h0:h0 + 8][:, :, None]
                            .broadcast_to((128, 8, 128)),
                            ALU.mult),
                        reads=(f"tok_f{cp}", "sb:g_trilm1"),
                        writes=(f"lhsb{gp}:{hh}",))

            def emit_diff(c, g):
                cp, gp = c % 2, g % 2
                for rnd in range(2):
                    for i in range(8):
                        hl = rnd * 8 + i
                        st("tensor", lambda e, hl=hl, i=i, gp=gp: e.matmul(
                            pdiff[:, i * 128:(i + 1) * 128],
                            lhsbP[gp][:, hl * 128:(hl + 1) * 128],
                            g_triu0[:, :], start=True, stop=True),
                            reads=(f"lhsb{gp}:{hl // 8}", "sb:g_triu0"),
                            writes=("pdiff",))
                    st("scalar", lambda e, rnd=rnd, gp=gp: e.activation(
                        t2P[gp][:, rnd * 1024:(rnd + 1) * 1024], pdiff[:, :],
                        AF.Exp, bias=0.0, scale=1.0),
                        reads=("pdiff",), writes=(f"t2_{gp}",))

            def emit_mbuild(c, g):
                cp, gp = c % 2, g % 2
                st("vector", lambda e, g=g, gp=gp, cp=cp: e.tensor_tensor(
                    mbatP[gp][:, :].rearrange("p (h l) -> p h l", h=16),
                    t2P[gp][:, :].rearrange("p (h l) -> p h l", h=16),
                    tok_fP[cp][:, 128 + g * 16:128 + (g + 1) * 16][:, :, None]
                    .broadcast_to((128, 16, CS)),
                    ALU.mult),
                    reads=(f"t2_{gp}", f"tok_f{cp}"), writes=(f"mbat{gp}",))
                st("vector", lambda e, g=g, gp=gp, cp=cp: e.tensor_tensor(
                    mbatP[gp][:, :].rearrange("p (h l) -> p h l", h=16),
                    mbatP[gp][:, :].rearrange("p (h l) -> p h l", h=16),
                    gmaskP[cp][:, g * 128:(g + 1) * 128][:, None, :]
                    .broadcast_to((128, 16, CS)),
                    ALU.mult),
                    reads=(f"mbat{gp}", f"gmask{cp}"), writes=(f"mbat{gp}",))

            def emit_ymms(c, g):
                cp, gp = c % 2, g % 2
                st("gpsimd", lambda e, g=g, gp=gp, cp=cp: e.tensor_tensor(
                    x_stP[gp][:, :].rearrange("p (h q) -> p h q", h=16),
                    x_tokP[cp][:, g * 1024:(g + 1) * 1024]
                    .rearrange("p (h q) -> p h q", h=16),
                    tok_fP[cp][:, 256 + g * 16:256 + (g + 1) * 16][:, :, None]
                    .broadcast_to((128, 16, P)),
                    ALU.mult),
                    reads=(f"x_tok{cp}", f"tok_f{cp}"), writes=(f"x_st{gp}",))
                for i in range(16):
                    h = g * 16 + i
                    st("tensor", lambda e, i=i, h=h, gp=gp, cp=cp: e.matmul(
                        pyD[:, i * 64:(i + 1) * 64],
                        mbatP[gp][:, i * 128:(i + 1) * 128],
                        x_tokP[cp][:, h * 64:(h + 1) * 64],
                        start=True, stop=True),
                        reads=(f"mbat{gp}", f"x_tok{cp}"), writes=("pyD",))
                    st("tensor", lambda e, i=i, h=h, g=g, cp=cp: e.matmul(
                        pyO[:, i * 64:(i + 1) * 64],
                        bcP[cp][:, 1024 + g * 128:1024 + (g + 1) * 128],
                        P_bf[:, h * 64:(h + 1) * 64],
                        start=True, stop=True),
                        reads=(f"bc_sb{cp}", "P_bf"), writes=("pyO",))
                for rnd in range(2):
                    for i in range(8):
                        hl = rnd * 8 + i
                        st("tensor", lambda e, hl=hl, i=i, g=g, gp=gp, cp=cp:
                            e.matmul(
                                pst[:, i * 64:(i + 1) * 64],
                                btokP[cp][:, g * 128:(g + 1) * 128],
                                x_stP[gp][:, hl * 64:(hl + 1) * 64],
                                start=True, stop=True),
                            reads=(f"btok{cp}", f"x_st{gp}"), writes=("pst",))
                    h0 = g * 16 + rnd * 8
                    st("gpsimd", lambda e, h0=h0, cp=cp: e.tensor_tensor(
                        ptmp[:, :].rearrange("p (h q) -> p h q", h=8),
                        Pst[:, h0 * 64:(h0 + 8) * 64]
                        .rearrange("p (h q) -> p h q", h=8),
                        drepP[cp][:, h0:h0 + 8][:, :, None]
                        .broadcast_to((128, 8, P)),
                        ALU.mult),
                        reads=("P", f"drep_c{cp}"), writes=("ptmp",))
                    st("vector", lambda e, h0=h0: e.tensor_add(
                        Pst[:, h0 * 64:(h0 + 8) * 64], ptmp[:, :],
                        pst[:, :]),
                        reads=("ptmp", "pst"), writes=("P",))

            def emit_assembly(c, g):
                cp, gp = c % 2, g % 2
                st("vector", lambda e, g=g, cp=cp: e.tensor_tensor(
                    t_off[:, :].rearrange("p (h q) -> p h q", h=16),
                    pyO[:, :].rearrange("p (h q) -> p h q", h=16),
                    e_tokP[cp][:, g * 16:(g + 1) * 16][:, :, None]
                    .broadcast_to((128, 16, P)),
                    ALU.mult),
                    reads=("pyO", f"e_tok{cp}"), writes=("t_off",))
                st("gpsimd", lambda e, g=g, cp=cp: e.tensor_tensor(
                    t_off2[:, :].rearrange("p (h q) -> p h q", h=16),
                    x_tokP[cp][:, g * 1024:(g + 1) * 1024]
                    .rearrange("p (h q) -> p h q", h=16),
                    d_rep[:, g * 16:(g + 1) * 16][:, :, None]
                    .broadcast_to((128, 16, P)),
                    ALU.mult),
                    reads=(f"x_tok{cp}", "d_rep"), writes=("t_off2",))
                st("vector", lambda e, g=g: e.tensor_add(
                    y_ch[:, g * 1024:(g + 1) * 1024], t_off[:, :],
                    pyD[:, :]),
                    reads=("t_off", "pyD"), writes=(f"y_ch:{g}",))
                st("gpsimd", lambda e, g=g: e.tensor_add(
                    y_ch[:, g * 1024:(g + 1) * 1024],
                    y_ch[:, g * 1024:(g + 1) * 1024], t_off2[:, :]),
                    reads=(f"y_ch:{g}", "t_off2"), writes=(f"y_ch:{g}",))

            p2pipe = bool(int(os.environ.get("K_P2_PIPE", "1")))
            emit_chunk_loads(0)
            emit_fields(0)
            for c in range(NCH):
                cp = c % 2
                if c + 1 < NCH:
                    emit_chunk_loads(c + 1)
                st("scalar", lambda e: e.activation(
                    P_bf[:, :], Pst[:, :], AF.Identity, bias=0.0, scale=1.0),
                    reads=("P",), writes=("P_bf",))
                if p2pipe:
                    emit_lhsb(c, 0)
                    emit_diff(c, 0)
                for g in range(8):
                    if p2pipe:
                        if g + 1 < 8:
                            emit_lhsb(c, g + 1)
                            emit_diff(c, g + 1)
                    else:
                        emit_lhsb(c, g)
                        emit_diff(c, g)
                    emit_mbuild(c, g)
                    emit_ymms(c, g)
                    emit_assembly(c, g)
                    if p2pipe and g == 3 and c + 1 < NCH:
                        emit_fields(c + 1)
                if not p2pipe and c + 1 < NCH:
                    emit_fields(c + 1)
                st("sync", lambda e, c=c: e.dma_start(
                    out=y1_tok.ap()[c * CS:(c + 1) * CS, :], in_=y_ch[:, :]),
                    reads=tuple(f"y_ch:{g}" for g in range(8)),
                    writes=(f"y1:{c}",), kind="dma", ch="sy1")

            st("sync", lambda e: e.dma_start(
                out=ccB_in.ap()[:, 0:INTER], in_=Pst[:, :]),
                reads=("P",), writes=("dram:ccB_in",), kind="dma", ch="ccB")
            st("sync", lambda e: e.dma_start(
                out=ccB_in.ap()[:, INTER:INTER + 1], in_=aseg[:, :],
                allow_slow_non_contiguous=True),
                reads=("aseg", "dram:ccB_in"), writes=("dram:ccB_in2",),
                kind="dma", ch="ccB")
            st("sync", lambda e: e.dma_start(
                out=sloc_d.ap()[:, :], in_=Pst[:, :]),
                reads=("P",), writes=("dram:sloc",), kind="dma", ch="sdbg")

            edec = S2("edec", [128, 128], F32)
            epre = [S2(f"epre{i}", [128, 128], F32) for i in range(2)]
            var_c = S2("var_c", [128, 1], F32)
            rstd_c = S2("rstd_c", [128, 1], F32)

            if not skip_cc:
                st("gpsimd", lambda e: e.collective_compute(
                    "AllGather", ALU.bypass,
                    replica_groups=[[0, 1, 2, 3], [4, 5, 6, 7]],
                    ins=[ccB_in.ap().opt()], outs=[ccB_out.ap().opt()]),
                    reads=("dram:ccB_in", "dram:ccB_in2"),
                    writes=("dram:ccB_out",), kind="cc")

            for j in range(4):
                st("sync", lambda e, j=j: e.dma_start(
                    out=tok_f[0:1, j * 128:(j + 1) * 128],
                    in_=ccB_out.ap()[j * 128:(j + 1) * 128,
                                     INTER:INTER + 1].transpose([1, 0]),
                    allow_slow_non_contiguous=True),
                    reads=("dram:ccB_out",), writes=("tok_f0",), kind="dma",
                    ch="lasg")

            st("vector", lambda e: e.memset(Pst[:, :], 0.0),
                reads=(), writes=("P",))
            sj_st = (f_dt, f_adt, f_wst, cmask)
            for j in range(4):
                st("vector", lambda e: e.memset(e_tok[0:1, :], 0.0),
                    reads=(), writes=("e_tok0",))
                for i in range(4):
                    st("vector", lambda e, j=j, i=i: e.scalar_tensor_tensor(
                        e_tok[0:1, :], tok_f[0:1, i * 128:(i + 1) * 128],
                        g_btw[:, j * 4 + i:j * 4 + i + 1], e_tok[0:1, :],
                        ALU.mult, ALU.add),
                        reads=("tok_f0", "sb:g_btw", "e_tok0"),
                        writes=("e_tok0",))
                st("scalar", lambda e: e.activation(
                    e_tok[0:1, :], e_tok[0:1, :], AF.Exp, bias=0.0,
                    scale=1.0),
                    reads=("e_tok0",), writes=("e_tok0",))
                st("tensor", lambda e: e.matmul(
                    pwk_f[:, 0:128], g_triu0[0:1, :], e_tok[0:1, :],
                    start=True, stop=True),
                    reads=("e_tok0", "sb:g_triu0"), writes=("pwk_f",))
                st("scalar", lambda e: e.activation(
                    edec[:, :], pwk_f[:, 0:128], AF.Identity, bias=0.0,
                    scale=1.0),
                    reads=("pwk_f",), writes=("edec",))
                for q in range(4):
                    sb_ = sj_st[q]
                    skey = ("f_dt", "f_adt", "f_wst", "cmask")[q]
                    st("sync", lambda e, j=j, q=q, sb_=sb_: e.dma_start(
                        out=sb_[:, :],
                        in_=ccB_out.ap()[j * 128:(j + 1) * 128,
                                         q * 2048:(q + 1) * 2048]),
                        reads=("dram:ccB_out",), writes=(skey,), kind="dma",
                        ch=f"lsj{q}")
                    st("gpsimd", lambda e, q=q, sb_=sb_: e.tensor_tensor(
                        sb_[:, :].rearrange("p (h x) -> p h x", h=32),
                        sb_[:, :].rearrange("p (h x) -> p h x", h=32),
                        edec[:, q * 32:(q + 1) * 32][:, :, None]
                        .broadcast_to((128, 32, P)),
                        ALU.mult),
                        reads=(skey, "edec"), writes=(skey,))
                    st("vector", lambda e, j=j, q=q, sb_=sb_: e.scalar_tensor_tensor(
                        Pst[:, q * 2048:(q + 1) * 2048], sb_[:, :],
                        g_inmask[:, j:j + 1],
                        Pst[:, q * 2048:(q + 1) * 2048],
                        ALU.mult, ALU.add),
                        reads=(skey, "sb:g_inmask", "P"), writes=("P",))

            st("vector", lambda e: e.memset(epre[0][:, :], 1.0),
                reads=(), writes=("epre0",))

            for c in range(NCH):
                pe_i, pe_o = epre[c % 2], epre[(c + 1) % 2]
                tt_c = c // 4
                bckeys = tuple(f"xbcT:{cc}:{tt_c}" for cc in range(72, 80))
                if tt_c == 0:
                    bckeys = bckeys + tuple(
                        f"xbcT:{cc}:0" for cc in range(72, 80))
                st("scalar", lambda e, c=c: e.dma_start(
                    out=bc_sb[:, 1024:2048].rearrange("p (g l) -> p g l", g=8),
                    in_=xbcT.ap()[INTER + 1024:INTER + 2048,
                                  c * CS:(c + 1) * CS]
                    .rearrange("(g p) l -> p g l", g=8)),
                    reads=bckeys, writes=("bc_sb0",), kind="dma", ch="bcC0",
                    label=f"bcC2_{c}")
                st("sync", lambda e, c=c: e.dma_start(
                    out=y_ch[:, :], in_=y1_tok.ap()[c * CS:(c + 1) * CS, :]),
                    reads=(f"y1:{c}",),
                    writes=("y_ch",) + tuple(f"y_ch:{g}" for g in range(8)),
                    kind="dma", ch="ly1", label=f"ly1_{c}")
                st("scalar", lambda e, c=c: e.dma_start_transpose(
                    x_tok[:, :], gateT.ap()[0:INTER, c * CS:(c + 1) * CS]),
                    reads=tuple(f"gateT:{o_}:{tt_c}" for o_ in range(64)),
                    writes=("x_tok0",), kind="dma", ch="lgate",
                    label=f"lgate{c}")
                st("scalar", lambda e: e.activation(
                    x_tokP[1][:, :], x_tok[:, :], AF.Silu, bias=0.0,
                    scale=1.0),
                    reads=("x_tok0",), writes=("x_tok1",))
                st("tensor", lambda e, c=c: e.transpose(
                    pwk_f[:, 384:512], f_acum[:, c * CS:(c + 1) * CS],
                    g_ident_f[:, :]),
                    reads=("f_acum", "sb:g_ident_f"), writes=("pwk_f",))
                st("vector", lambda e: e.tensor_copy(
                    tok_f[:, 384:512], pwk_f[:, 384:512]),
                    reads=("pwk_f",), writes=("tok_f0",))
                st("scalar", lambda e: e.activation(
                    e_tok[:, :], tok_f[:, 384:512], AF.Exp, bias=0.0,
                    scale=1.0),
                    reads=("tok_f0",), writes=("e_tok0",))
                st("tensor", lambda e: e.matmul(
                    pwk_f[:, 0:128], g_sel_last[:, :], tok_f[:, 384:512],
                    start=True, stop=True),
                    reads=("sb:g_sel_last", "tok_f0"), writes=("pwk_f",))
                st("scalar", lambda e: e.activation(
                    drep_c[:, :], pwk_f[:, 0:128], AF.Exp, bias=0.0,
                    scale=1.0),
                    reads=("pwk_f",), writes=("drep_c0",))
                st("vector", lambda e, pe_i=pe_i: e.tensor_tensor(
                    xre(P_bf[:, :]), xre(Pst[:, :]),
                    pe_i[:, :][:, :, None].broadcast_to((128, H, P)),
                    ALU.mult),
                    reads=("P", f"epre{c % 2}"), writes=("P_bf",))
                st("gpsimd", lambda e, pe_i=pe_i, pe_o=pe_o: e.tensor_mul(
                    pe_o[:, :], pe_i[:, :], drep_c[:, :]),
                    reads=(f"epre{c % 2}", "drep_c0"),
                    writes=(f"epre{(c + 1) % 2}",))

                for g in range(8):
                    for i in range(16):
                        h = g * 16 + i
                        st("tensor", lambda e, i=i, h=h, g=g: e.matmul(
                            pyO[:, i * 64:(i + 1) * 64],
                            bc_sb[:, 1024 + g * 128:1024 + (g + 1) * 128],
                            P_bf[:, h * 64:(h + 1) * 64],
                            start=True, stop=True),
                            reads=("bc_sb0", "P_bf"), writes=("pyO",))
                    st("vector", lambda e, g=g: e.tensor_tensor(
                        t_off[:, :].rearrange("p (h q) -> p h q", h=16),
                        pyO[:, :].rearrange("p (h q) -> p h q", h=16),
                        e_tok[:, g * 16:(g + 1) * 16][:, :, None]
                        .broadcast_to((128, 16, P)),
                        ALU.mult),
                        reads=("pyO", "e_tok0"), writes=("t_off",))
                    st("gpsimd", lambda e, g=g: e.tensor_add(
                        y_ch[:, g * 1024:(g + 1) * 1024],
                        y_ch[:, g * 1024:(g + 1) * 1024], t_off[:, :]),
                        reads=("y_ch", "t_off"), writes=("y_ch",))

                st("scalar", lambda e: e.activation(
                    mbat[:, :], y_ch[:, 0:2048], AF.Square, bias=0.0,
                    scale=1.0, accum_out=var_c[:, :]),
                    reads=("y_ch",), writes=("mbat0", "var_c"))
                for qq in range(1, 4):
                    st("scalar", lambda e, qq=qq: e.activation(
                        mbat[:, :], y_ch[:, qq * 2048:(qq + 1) * 2048],
                        AF.Square, bias=0.0, scale=1.0,
                        accum_out=rstd_c[:, :]),
                        reads=("y_ch",), writes=("mbat0", "rstd_c"))
                    st("vector", lambda e: e.tensor_add(
                        var_c[:, :], var_c[:, :], rstd_c[:, :]),
                        reads=("var_c", "rstd_c"), writes=("var_c",))
                st("vector", lambda e: e.tensor_scalar(
                    var_c[:, :], var_c[:, :], 1.0 / INTER, EPS,
                    ALU.mult, ALU.add),
                    reads=("var_c",), writes=("var_c",))
                st("scalar", lambda e: e.activation(
                    var_c[:, :], var_c[:, :], AF.Sqrt, bias=0.0, scale=1.0),
                    reads=("var_c",), writes=("var_c",))
                st("vector", lambda e: e.reciprocal(rstd_c[:, :], var_c[:, :]),
                    reads=("var_c",), writes=("rstd_c",))

                for g in range(8):
                    st("vector", lambda e, g=g: e.tensor_scalar(
                        t_off[:, :], y_ch[:, g * 1024:(g + 1) * 1024],
                        rstd_c[:, :], None, ALU.mult),
                        reads=("y_ch", "rstd_c"),
                        writes=("t_off",))
                    st("gpsimd", lambda e, g=g: e.tensor_mul(
                        y_ch[:, g * 1024:(g + 1) * 1024], t_off[:, :],
                        x_tokP[1][:, g * 1024:(g + 1) * 1024]),
                        reads=("t_off", "x_tok1"), writes=("y_ch",))

                st("sync", lambda e, c=c: e.dma_start(
                    out=y1_tok.ap()[c * CS:(c + 1) * CS, :], in_=y_ch[:, :]),
                    reads=("y_ch",), writes=(f"y2:{c}",), kind="dma",
                    ch="sy1")

            ytslots = [(t2, 0), (t2, 1024), (mbat, 0), (mbat, 1024)]
            for th in range(2):
                ykeys = tuple(f"y2:{c}" for c in range(th * 8, th * 8 + 8))
                for kc in range(64):
                    buf, off = ytslots[kc % 4]
                    bkey = "t2_0" if kc % 4 < 2 else "mbat0"
                    skey = f"yts{kc % 4}"
                    rch = f"yt{kc % 4}"
                    reng = "scalar" if kc % 2 == 0 else "sync"
                    st(reng, lambda e, th=th, kc=kc, buf=buf, off=off:
                        e.dma_start_transpose(
                            buf[:, off:off + 1024],
                            y1_tok.ap()[th * 1024:(th + 1) * 1024,
                                        kc * 128:(kc + 1) * 128]),
                        reads=ykeys, writes=(skey, bkey), kind="dma", ch=rch)
                    st(reng, lambda e, th=th, kc=kc, buf=buf, off=off:
                        e.dma_start(
                            out=y2T_d.ap()[kc * 128:(kc + 1) * 128,
                                           th * 1024:(th + 1) * 1024],
                            in_=buf[:, off:off + 1024]),
                        reads=(skey, bkey), writes=(f"y2T:{kc}:{th}",),
                        kind="dma", ch=rch)

        p.fence()

        ph3 = ExitStack()
        with ph3:
            S3 = lambda n, sh, dt=F32: ph3.enter_context(
                nc.sbuf_tensor(n, list(sh), dt))
            obuf = S3("obuf", [128, 32 * 1024], BF16)
            x1buf = S3("x1buf", [128, 32 * 1024], BF16)
            wk_f = [S3(f"wk_f{i}", [128, 2048]) for i in range(2)]
            wk_b = [S3(f"wk_b{i}", [128, 2048], BF16) for i in range(2)]
            yr = [S3(f"yr{i}", [128, 4096], BF16) for i in range(2)]
            x0s = [S3(f"x0s{i}", [128, 512]) for i in range(2)]
            sqb = [S3(f"sqb{i}", [128, 512], BF16) for i in range(2)]
            ones_b = S3("ones_b", [128, 1], BF16)
            m_row = S3("m_row", [1, 512])
            v_row = S3("v_row", [1, 512])
            t_row = S3("t_row", [1, 512])
            awb = S3("awb", [128, 512], BF16)
            rwb = S3("rwb", [128, 256], BF16)
            aws = wk_f[0][:, 0:512]
            rws = wk_f[0][:, 512:768]
            axrt = S3("axrt", [40, 1024])
            axf = axrt[0:16, :]
            rtf = axrt[32:40, :]
            ptok = S3("ptok", [128, 64])
            pcol = S3("pcol", [128, 1])
            atok = S3("atok", [128, 128])
            w2b = S3("w2b", [128, 128])
            w2ts = S3("w2ts", [128, 128], BF16)
            mixb = S3("mixb", [16, 1024], BF16)
            coresb = S3("coresb", [128, 16], BF16)
            bwb = S3("bwb", [16, 4096], BF16)
            outs = [S3(f"outs{i}", [128, 512]) for i in range(2)]
            nw_sb = S3("nw_sb", [128, 64])
            pmm = [ph3.enter_context(nc.psum_tensor(f"pmm{i}", [128, 1024], F32))
                   for i in range(4)]
            st = p.step

            st("vector", lambda e: e.tensor_copy(ones_b[:, :], g_ones_col[:, :]),
                reads=("sb:g_ones_col",), writes=("ones_b",))
            st("sync", lambda e: e.dma_start(
                out=nw_sb[:, :], in_=normw_cols.ap()),
                reads=(), writes=("nw_sb",), kind="dma", ch="lnw2")
            st("vector", lambda e: e.tensor_copy(coresb[:, :], coresf_sb[:, :]),
                reads=("sb:coresf_sb",), writes=("coresb",))
            for hf in range(2):
                st("sync", lambda e, hf=hf: e.dma_start(
                    out=wk_f[0][0:16, :],
                    in_=bw_t.ap()[:, hf * 2048:(hf + 1) * 2048]),
                    reads=("wkf0",), writes=("wkf0",), kind="dma", ch="wk0")
                st("vector", lambda e, hf=hf: e.tensor_copy(
                    bwb[:, hf * 2048:(hf + 1) * 2048], wk_f[0][0:16, :]),
                    reads=("wkf0",), writes=("bwb",))

            def stats_and_norm(vsrc, dst_fc, th, out_f32, okeys):
                for ts_ in range(2):
                    tkey = f"st{th}{ts_}"
                    for fc in range(32):
                        sq = sqb[fc % 2]
                        st("scalar", lambda e, fc=fc, ts_=ts_, sq=sq: e.activation(
                            sq[:, :], vsrc(fc, ts_), AF.Square, bias=0.0,
                            scale=1.0),
                            reads=okeys, writes=(f"sqb{fc % 2}",))
                        st("tensor", lambda e, fc=fc, ts_=ts_: e.matmul(
                            pmm[0][0:1, 0:512], ones_b[:, :], vsrc(fc, ts_),
                            start=(fc == 0), stop=(fc == 31)),
                            reads=okeys + ("ones_b",), writes=("pmm0",))
                        st("tensor", lambda e, fc=fc, sq=sq: e.matmul(
                            pmm[0][32:33, 512:1024], ones_b[:, :], sq[:, :],
                            start=(fc == 0), stop=(fc == 31)),
                            reads=(f"sqb{fc % 2}", "ones_b"), writes=("pmm0",))
                    st("vector", lambda e: e.tensor_scalar(
                        m_row[:, :], pmm[0][0:1, 0:512], 1.0 / DIM, None,
                        ALU.mult),
                        reads=("pmm0",), writes=("m_row",))
                    st("vector", lambda e: e.tensor_scalar(
                        v_row[:, :], pmm[0][32:33, 512:1024], 1.0 / DIM, None,
                        ALU.mult),
                        reads=("pmm0",), writes=("v_row",))
                    st("scalar", lambda e: e.activation(
                        t_row[:, :], m_row[:, :], AF.Square, bias=0.0,
                        scale=1.0),
                        reads=("m_row",), writes=("t_row",))
                    st("vector", lambda e: e.tensor_sub(
                        v_row[:, :], v_row[:, :], t_row[:, :]),
                        reads=("v_row", "t_row"), writes=("v_row",))
                    st("vector", lambda e: e.tensor_scalar(
                        v_row[:, :], v_row[:, :], EPS, None, ALU.add),
                        reads=("v_row",), writes=("v_row",))
                    st("scalar", lambda e: e.activation(
                        v_row[:, :], v_row[:, :], AF.Sqrt, bias=0.0,
                        scale=1.0),
                        reads=("v_row",), writes=("v_row",))
                    st("vector", lambda e: e.reciprocal(
                        t_row[:, :], v_row[:, :]),
                        reads=("v_row",), writes=("t_row",))
                    st("tensor", lambda e: e.matmul(
                        pmm[1][:, 0:512], g_triu0[0:1, :], m_row[:, :],
                        start=True, stop=True),
                        reads=("m_row", "sb:g_triu0"), writes=("pmm1",))
                    st("tensor", lambda e: e.matmul(
                        pmm[1][:, 512:1024], g_triu0[0:1, :], t_row[:, :],
                        start=True, stop=True),
                        reads=("t_row", "sb:g_triu0"), writes=("pmm1",))
                    for fc in range(32):
                        dst_fc(fc, ts_)

            for th in range(2):
                tsl = slice(th * 1024, (th + 1) * 1024)
                for og in range(8):
                    for kg in range(16):
                        sl = kg % 2
                        st("sync", lambda e, og=og, kg=kg, sl=sl: e.dma_start(
                            out=wk_f[sl][:, :],
                            in_=wout_g3.ap()[og:og + 1, kg:kg + 1, :, :]
                            .squeeze(0).squeeze(0)),
                            reads=(), writes=(f"wkf{sl}",), kind="dma",
                            ch=f"wk{sl}", label=f"wo{th}_{og}_{kg}")
                        for j in range(4):
                            st("vector", lambda e, sl=sl, kg=kg, j=j:
                                e.tensor_scalar(
                                    wk_b[sl][:, j * 512:(j + 1) * 512],
                                    wk_f[sl][:, j * 512:(j + 1) * 512],
                                    nw_sb[:, kg * 4 + j:kg * 4 + j + 1],
                                    None, ALU.mult),
                                reads=(f"wkf{sl}", "nw_sb"),
                                writes=(f"wkb{sl}",))
                        ykeys = tuple(f"y2T:{kc}:{th}" for kc in
                                      range(kg * 4, kg * 4 + 4))
                        st("scalar", lambda e, kg=kg, th=th, sl=sl: e.dma_start(
                            out=yr[sl][:, :].rearrange(
                                "p (j t) -> p j t", j=4),
                            in_=y2T_d.ap()[kg * 512:(kg + 1) * 512,
                                           th * 1024:(th + 1) * 1024]
                            .rearrange("(j p) t -> p j t", j=4)),
                            reads=ykeys, writes=(f"yr{sl}",), kind="dma",
                            ch=f"ys{sl}", label=f"ys{th}_{og}_{kg}")
                        for j in range(4):
                            for i in range(4):
                                for ts_ in range(2):
                                    st("tensor", lambda e, i=i, j=j, ts_=ts_,
                                       sl=sl, kg=kg: e.matmul(
                                        pmm[i][:, ts_ * 512:(ts_ + 1) * 512],
                                        wk_b[sl][:, (j * 4 + i) * 128:
                                                 (j * 4 + i + 1) * 128],
                                        yr[sl][:, j * 1024 + ts_ * 512:
                                               j * 1024 + (ts_ + 1) * 512],
                                        start=(kg == 0 and j == 0),
                                        stop=(kg == 15 and j == 3)),
                                        reads=(f"wkb{sl}", f"yr{sl}"),
                                        writes=(f"pmm{i}",))
                    for i in range(4):
                        o_ = og * 4 + i
                        st("vector", lambda e, i=i, o_=o_: e.tensor_copy(
                            obuf[:, o_ * 1024:(o_ + 1) * 1024], pmm[i][:, :]),
                            reads=(f"pmm{i}",), writes=(f"ob:{o_}",))

                for fc in range(32):
                    for ts_ in range(2):
                        xs_ = x0s[ts_ % 2]
                        st("sync", lambda e, fc=fc, ts_=ts_, th=th, xs_=xs_:
                            e.dma_start(
                                out=xs_[:, :],
                                in_=x_inT.ap()[fc * 128:(fc + 1) * 128,
                                               th * 1024 + ts_ * 512:
                                               th * 1024 + (ts_ + 1) * 512]),
                            reads=(), writes=(f"x0s{ts_ % 2}",), kind="dma",
                            ch=f"x0c{ts_ % 2}")
                        st("vector", lambda e, fc=fc, ts_=ts_, xs_=xs_:
                            e.tensor_add(
                                obuf[:, fc * 1024 + ts_ * 512:
                                     fc * 1024 + (ts_ + 1) * 512],
                                obuf[:, fc * 1024 + ts_ * 512:
                                     fc * 1024 + (ts_ + 1) * 512],
                                xs_[:, :]),
                            reads=(f"ob:{fc}", f"x0s{ts_ % 2}"),
                            writes=(f"ob:{fc}",))

                okeys1 = tuple(f"ob:{fc}" for fc in range(32))

                def vsrc1(fc, ts_):
                    return obuf[:, fc * 1024 + ts_ * 512:
                                fc * 1024 + (ts_ + 1) * 512]

                def dst1(fc, ts_):
                    st("vector", lambda e, fc=fc, ts_=ts_: e.tensor_sub(
                        x1buf[:, fc * 1024 + ts_ * 512:
                              fc * 1024 + (ts_ + 1) * 512],
                        vsrc1(fc, ts_), pmm[1][:, 0:512]),
                        reads=okeys1 + ("pmm1",), writes=(f"x1:{fc}",))
                    st("vector", lambda e, fc=fc, ts_=ts_: e.tensor_mul(
                        x1buf[:, fc * 1024 + ts_ * 512:
                              fc * 1024 + (ts_ + 1) * 512],
                        x1buf[:, fc * 1024 + ts_ * 512:
                              fc * 1024 + (ts_ + 1) * 512],
                        pmm[1][:, 512:1024]),
                        reads=(f"x1:{fc}", "pmm1"), writes=(f"x1:{fc}",))

                stats_and_norm(vsrc1, dst1, th, False, okeys1)
                x1keys = tuple(f"x1:{fc}" for fc in range(32))

                st("sync", lambda e: e.dma_start(
                    out=aws, in_=aw_kt.ap()),
                    reads=("wkf0",), writes=("wkf0",), kind="dma",
                    ch="wk0")
                st("sync", lambda e: e.dma_start(
                    out=rws, in_=rw_kt.ap()),
                    reads=("wkf0",), writes=("wkf0",), kind="dma",
                    ch="wk0")
                st("vector", lambda e: e.tensor_copy(awb[:, :], aws),
                    reads=("wkf0",), writes=("awb",))
                st("vector", lambda e: e.tensor_copy(rwb[:, :], rws),
                    reads=("wkf0",), writes=("rwb",))
                for kc in range(32):
                    for ts_ in range(2):
                        st("tensor", lambda e, kc=kc, ts_=ts_: e.matmul(
                            pmm[2][0:16, ts_ * 512:(ts_ + 1) * 512],
                            awb[:, kc * 16:(kc + 1) * 16],
                            x1buf[:, kc * 1024 + ts_ * 512:
                                  kc * 1024 + (ts_ + 1) * 512],
                            start=(kc == 0), stop=(kc == 31)),
                            reads=("awb",) + x1keys, writes=("pmm2",))
                        st("tensor", lambda e, kc=kc, ts_=ts_: e.matmul(
                            pmm[3][0:8, ts_ * 512:(ts_ + 1) * 512],
                            rwb[:, kc * 8:(kc + 1) * 8],
                            x1buf[:, kc * 1024 + ts_ * 512:
                                  kc * 1024 + (ts_ + 1) * 512],
                            start=(kc == 0), stop=(kc == 31)),
                            reads=("rwb",) + x1keys, writes=("pmm3",))
                st("vector", lambda e: e.tensor_copy(
                    axf[:, :], pmm[2][0:16, :]),
                    reads=("pmm2",), writes=("axf",))
                st("vector", lambda e: e.tensor_copy(
                    rtf[:, :], pmm[3][0:8, :]),
                    reads=("pmm3",), writes=("rtf",))

                for i in range(8):
                    st("tensor", lambda e, i=i: e.transpose(
                        pmm[3][:, i * 8:(i + 1) * 8], rtf[:, i * 128:(i + 1) * 128],
                        g_ident_f[32:40, 32:40]),
                        reads=("rtf", "sb:g_ident_f"), writes=("pmm3",))
                    st("tensor", lambda e, i=i: e.transpose(
                        pmm[3][:, 64 + i * 16:64 + (i + 1) * 16],
                        axf[:, i * 128:(i + 1) * 128],
                        g_ident_f[0:16, 0:16]),
                        reads=("axf", "sb:g_ident_f"), writes=("pmm3",))
                for i in range(8):
                    st("vector", lambda e, i=i: e.reduce_max(
                        pcol[:, :], pmm[3][:, i * 8:(i + 1) * 8], axis=AX.X,
                        negate=True),
                        reads=("pmm3",), writes=("pcol",))
                    st("scalar", lambda e, i=i: e.activation(
                        ptok[:, i * 8:(i + 1) * 8], pmm[3][:, i * 8:(i + 1) * 8],
                        AF.Exp, bias=pcol[:, :], scale=1.0),
                        reads=("pmm3", "pcol"), writes=(f"pt:{i}",))
                    st("vector", lambda e, i=i: e.reduce_sum(
                        pcol[:, :], ptok[:, i * 8:(i + 1) * 8], axis=AX.X),
                        reads=(f"pt:{i}",), writes=("pcol",))
                    st("vector", lambda e, i=i: e.reciprocal(
                        pcol[:, :], pcol[:, :]),
                        reads=("pcol",), writes=("pcol",))
                    st("vector", lambda e, i=i: e.tensor_scalar(
                        ptok[:, i * 8:(i + 1) * 8], ptok[:, i * 8:(i + 1) * 8],
                        pcol[:, :], None, ALU.mult),
                        reads=(f"pt:{i}", "pcol"), writes=(f"pt:{i}",))
                    st("vector", lambda e, i=i: e.tensor_copy(
                        atok[:, 0:16], pmm[3][:, 64 + i * 16:64 + (i + 1) * 16]),
                        reads=("pmm3",), writes=("atok",))
                    st("vector", lambda e, i=i: e.tensor_tensor(
                        w2b[:, :].rearrange("p (a b) -> p a b", a=8),
                        ptok[:, i * 8:(i + 1) * 8][:, :, None]
                        .broadcast_to((128, 8, 16)),
                        atok[:, 0:16][:, None, :].broadcast_to((128, 8, 16)),
                        ALU.mult),
                        reads=(f"pt:{i}", "atok"), writes=("w2b",))
                    st("tensor", lambda e, i=i: e.transpose(
                        pmm[3][:, 192:320], w2b[:, :], g_ident_f[:, :]),
                        reads=("w2b", "sb:g_ident_f"), writes=("pmm3",))
                    st("vector", lambda e, i=i: e.tensor_copy(
                        w2ts[:, :], pmm[3][:, 192:320]),
                        reads=("pmm3",), writes=("w2ts",))
                    st("tensor", lambda e, i=i: e.matmul(
                        pmm[2][0:16, i * 128:(i + 1) * 128], coresb[:, :],
                        w2ts[:, :], start=True, stop=True),
                        reads=("coresb", "w2ts"), writes=("pmm2",))
                st("scalar", lambda e: e.activation(
                    mixb[:, :], pmm[2][0:16, :], AF.Identity, bias=0.0,
                    scale=SCALING),
                    reads=("pmm2",), writes=("mixb",))

                for og in range(8):
                    for kg in range(8):
                        sl = kg % 2
                        st("sync", lambda e, og=og, kg=kg, sl=sl: e.dma_start(
                            out=wk_f[sl][:, :],
                            in_=wbase_g3.ap()[og:og + 1, kg:kg + 1, :, :]
                            .squeeze(0).squeeze(0)),
                            reads=(), writes=(f"wkf{sl}",), kind="dma",
                            ch=f"wk{sl}", label=f"wb{th}_{og}_{kg}")
                        ceng = "gpsimd" if kg % 2 == 0 else "vector"
                        st(ceng, lambda e, sl=sl: e.tensor_copy(
                            wk_b[sl][:, :], wk_f[sl][:, :]),
                            reads=(f"wkf{sl}",), writes=(f"wkb{sl}",))
                        for j in range(4):
                            kc = kg * 4 + j
                            for i in range(4):
                                for ts_ in range(2):
                                    st("tensor", lambda e, i=i, j=j, ts_=ts_,
                                       sl=sl, kc=kc, kg=kg: e.matmul(
                                        pmm[i][:, ts_ * 512:(ts_ + 1) * 512],
                                        wk_b[sl][:, (j * 4 + i) * 128:
                                                 (j * 4 + i + 1) * 128],
                                        x1buf[:, kc * 1024 + ts_ * 512:
                                              kc * 1024 + (ts_ + 1) * 512],
                                        start=(kg == 0 and j == 0),
                                        stop=False),
                                        reads=(f"wkb{sl}",) + x1keys,
                                        writes=(f"pmm{i}",))
                    for i in range(4):
                        o_ = og * 4 + i
                        for ts_ in range(2):
                            st("tensor", lambda e, i=i, o_=o_, ts_=ts_: e.matmul(
                                pmm[i][:, ts_ * 512:(ts_ + 1) * 512],
                                bwb[0:16, o_ * 128:(o_ + 1) * 128],
                                mixb[:, ts_ * 512:(ts_ + 1) * 512],
                                start=False, stop=True),
                                reads=("bwb", "mixb"), writes=(f"pmm{i}",))
                        st("scalar", lambda e, i=i, o_=o_: e.activation(
                            obuf[:, o_ * 1024:(o_ + 1) * 1024], pmm[i][:, :],
                            AF.Identity, bias=g_bbase[:, o_:o_ + 1],
                            scale=1.0),
                            reads=(f"pmm{i}", "sb:g_bbase"),
                            writes=(f"ob:{o_}",))

                for fc in range(32):
                    st("gpsimd", lambda e, fc=fc: e.tensor_add(
                        obuf[:, fc * 1024:(fc + 1) * 1024],
                        obuf[:, fc * 1024:(fc + 1) * 1024],
                        x1buf[:, fc * 1024:(fc + 1) * 1024]),
                        reads=(f"ob:{fc}", f"x1:{fc}"), writes=(f"ob:{fc}",))

                def dst2(fc, ts_, th=th):
                    ob_ = outs[ts_ % 2]
                    st("vector", lambda e, fc=fc, ts_=ts_, ob_=ob_: e.tensor_sub(
                        ob_[:, :], vsrc1(fc, ts_), pmm[1][:, 0:512]),
                        reads=(f"ob:{fc}", "pmm1"), writes=(f"outs{ts_ % 2}",))
                    st("vector", lambda e, fc=fc, ts_=ts_, ob_=ob_: e.tensor_mul(
                        ob_[:, :], ob_[:, :], pmm[1][:, 512:1024]),
                        reads=(f"outs{ts_ % 2}", "pmm1"),
                        writes=(f"outs{ts_ % 2}",))
                    st("sync", lambda e, fc=fc, ts_=ts_, th=th, ob_=ob_:
                        e.dma_start(
                            out=out_t.ap()[fc * 128:(fc + 1) * 128,
                                           th * 1024 + ts_ * 512:
                                           th * 1024 + (ts_ + 1) * 512],
                            in_=ob_[:, :]),
                        reads=(f"outs{ts_ % 2}",),
                        writes=(f"out:{th}:{fc}:{ts_}",), kind="dma",
                        ch=f"sout{ts_ % 2}")

                stats_and_norm(vsrc1, dst2, th, True,
                               tuple(f"ob:{fc}" for fc in range(32)))

        p.fence()

        p.fence()
        dbg_map = dict(gateT=gateT, xbcT=xbcT, dtT_d=dtT_d, y1_tok=y1_tok,
                       ccA_out=ccA_out, sloc_d=sloc_d, ccB_out=ccB_out,
                       y2T_d=y2T_d)
        for name in debug:
            src = dbg_map[name]
            sh = list(src.ap().shape)
            o = nc.dram_tensor("dbg_" + name, sh, src.ap().dtype,
                               kind="ExternalOutput")
            nrow, ncol = sh
            bb = ctx.enter_context(nc.sbuf_tensor(
                "dbgb_" + name, [128, min(ncol, 2048)], src.ap().dtype))
            bar = ()
            for r0 in range(0, nrow, 128):
                rr = min(128, nrow - r0)
                for c0 in range(0, ncol, 2048):
                    cw = min(2048, ncol - c0)
                    p.step("sync", lambda e, r0=r0, c0=c0, rr=rr, cw=cw,
                           src=src, bb=bb: e.dma_start(
                        out=bb[0:rr, 0:cw], in_=src.ap()[r0:r0 + rr, c0:c0 + cw]),
                        reads=bar, writes=("dbgb:" + name,), kind="dma",
                        ch="dbg_" + name)
                    p.step("sync", lambda e, r0=r0, c0=c0, rr=rr, cw=cw,
                           o=o, bb=bb: e.dma_start(
                        out=o.ap()[r0:r0 + rr, c0:c0 + cw], in_=bb[0:rr, 0:cw]),
                        reads=("dbgb:" + name,), writes=("dbgo:" + name,),
                        kind="dma", ch="dbg_" + name)

        p.fence()
        p.step("sync", lambda e: e.nop(), reads=(), writes=())

        sems = {}
        for name in p.sem_names():
            sems[name] = ctx.enter_context(nc.semaphore(name))
        p.emit(nc, block, sems)
    return nc, din



def _prep_shared(L0_attn_in_proj_w, L0_attn_conv_w, L0_attn_conv_b,
                 L0_attn_dt_bias, L0_attn_A_log, L0_attn_D, L0_attn_norm_w,
                 L0_attn_out_proj_w, L0_ffn_W_base, L0_ffn_b_base, L0_ffn_A_w,
                 L0_ffn_B_w, L0_ffn_router_w, L0_ffn_cores):
    f = np.float32
    sh = {}
    w = np.ascontiguousarray(L0_attn_in_proj_w, f)
    sh["win_t"] = np.ascontiguousarray(
        w.reshape(145, 128, 32, 128).transpose(0, 3, 2, 1))
    wo = np.ascontiguousarray(L0_attn_out_proj_w, f)
    sh["wout_g3"] = np.ascontiguousarray(
        wo.reshape(8, 4, 128, 16, 4, 128).transpose(0, 3, 5, 4, 1, 2)
        .reshape(8, 16, 128, 2048))
    wb = np.ascontiguousarray(L0_ffn_W_base, f)
    sh["wbase_g3"] = np.ascontiguousarray(
        wb.reshape(8, 4, 128, 8, 4, 128).transpose(0, 3, 5, 4, 1, 2)
        .reshape(8, 8, 128, 2048))
    aw = np.ascontiguousarray(L0_ffn_A_w, f)
    sh["aw_kt"] = np.ascontiguousarray(
        aw.T.reshape(32, 128, 16).transpose(1, 0, 2).reshape(128, 512))
    rw = np.ascontiguousarray(L0_ffn_router_w, f)
    sh["rw_kt"] = np.ascontiguousarray(
        rw.T.reshape(32, 128, 8).transpose(1, 0, 2).reshape(128, 256))
    sh["bw_t"] = np.ascontiguousarray(L0_ffn_B_w.T, f)
    sh["coresf"] = np.ascontiguousarray(
        np.asarray(L0_ffn_cores, f).reshape(128, 16))
    cw = np.asarray(L0_attn_conv_w, f).reshape(CONV_DIM, 4)
    sh["conv_w_r"] = np.ascontiguousarray(
        cw.reshape(80, 128, 4).transpose(1, 0, 2).reshape(128, 320))
    cb = np.asarray(L0_attn_conv_b, f)
    sh["conv_b_r"] = np.ascontiguousarray(cb.reshape(80, 128).T)
    sh["dt_bias_c"] = np.asarray(L0_attn_dt_bias, f).reshape(128, 1)
    sh["a_log_c"] = np.asarray(L0_attn_A_log, f).reshape(128, 1)
    sh["d_exp_row"] = np.ascontiguousarray(
        np.repeat(np.asarray(L0_attn_D, f), P).reshape(1, INTER))
    sh["d_row"] = np.asarray(L0_attn_D, f).reshape(1, 128)
    sh["normw_row"] = np.asarray(L0_attn_norm_w, f).reshape(1, INTER)
    sh["normw_cols"] = np.ascontiguousarray(
        np.asarray(L0_attn_norm_w, f).reshape(64, 128).T)
    sh["bbase_cols"] = np.ascontiguousarray(
        np.asarray(L0_ffn_b_base, f).reshape(32, 128).T)
    sh["ident_f"] = np.eye(128, dtype=f)
    sh["trilm1_f"] = np.tril(np.ones((128, 128), f), -1)
    sh["triu0_f"] = np.triu(np.ones((128, 128), f), 0)
    sl = np.zeros((128, 128), f)
    sl[127, :] = 1.0
    sh["sel_last_f"] = sl
    sh["ones_col"] = np.ones((128, 1), f)
    return sh


def _prep_core(core, hidden):
    f = np.float32
    b = core // 4
    pos = core % 4
    s0 = pos * T
    d = {}
    d["x_inT"] = np.ascontiguousarray(hidden[b, s0:s0 + T, :].T, f)
    pc = np.zeros((128, 4), f)
    if pos > 0:
        pc[:, pos - 1] = 1.0
    d["prev_cols"] = pc
    im = np.zeros((128, 4), f)
    im[:, :pos] = 1.0
    d["inmask_cols"] = im
    btw = np.zeros((1, 16), f)
    for j in range(4):
        for i in range(4):
            if j < i < pos:
                btw[0, j * 4 + i] = 1.0
    d["btw_row"] = btw
    return d


def kernel(**inputs):
    nc, _ = build()
    sh = _prep_shared(**{k: v for k, v in inputs.items()
                         if k != "L0_attn_hidden_states"})
    hidden = np.asarray(inputs["L0_attn_hidden_states"], np.float32)
    in_maps = []
    for c in range(NCORES):
        m = dict(sh)
        m.update(_prep_core(c, hidden))
        in_maps.append(m)
    res = run_bass_kernel_spmd(nc, in_maps, core_ids=list(range(NCORES)),
                               trace_cores=[0, 1, 2, 3])
    out = np.empty((2, 8192, DIM), np.float32)
    for c in range(NCORES):
        b, pos = c // 4, c % 4
        out[b, pos * T:(pos + 1) * T, :] = res.results[c]["out_t"].T
    return out
